# version 10
# speedup vs baseline: 1.4201x; 1.0062x over previous
import contextlib
import numpy as np
import concourse.bass as bass
import concourse.mybir as mybir
from concourse.bass_types import AP
from concourse.library_config import mlp as LIB_MLP, local_scatter as LIB_LS
from concourse.library_overlay import lower_extended_insts
from concourse.bass_utils import run_bass_kernel_spmd

F32 = mybir.dt.float32
I16 = mybir.dt.int16
OP = mybir.AluOpType
AF = mybir.ActivationFunctionType
AX = mybir.AxisListType.X

B2 = 2
S = 2048
D = 1024
N = 16
E = 8
HID = 4096
CAP = 513
NPAD = 528
C = 128
J = S // C
XROW = 1088
SCALE = 0.25
EPS = 1e-5
ENGS = ("sync", "tensor", "vector", "scalar", "gpsimd")


def host_consts():
    ident = np.eye(128, dtype=np.float32)
    iota_mat = np.tile(np.arange(128, dtype=np.float32)[None, :], (128, 1))
    iota1_mat = np.tile(np.arange(0, 128, dtype=np.float32)[None, :], (128, 8)).reshape(128, 1024)
    reviota_col = (127.0 - np.arange(128, dtype=np.float32))[:, None].copy()
    ones_row = np.ones((1, 128), dtype=np.float32)
    iota16 = np.ascontiguousarray(np.tile(np.arange(S, dtype=np.int16)[None, :], (16, 1)))
    bo8 = np.zeros((128, 16), dtype=np.float32)
    for pp in range(128):
        bo8[pp, pp // 8] = 1.0
    bo8t = np.ascontiguousarray(bo8.T)
    ones16 = np.ones((16, S), dtype=np.float32)
    return dict(c_ident=ident, c_iota=iota_mat, c_iota1=iota1_mat,
                c_reviota=reviota_col, c_ones_row=ones_row, c_iota16=iota16,
                c_bo8=bo8, c_bo8t=bo8t, c_ones16=ones16)


class Prog:
    def __init__(self, nc, stack, sem_names):
        self.nc = nc
        self.thunks = []
        self.h = {n: stack.enter_context(nc.semaphore(n)) for n in sem_names}
        self.v = {n: 0 for n in sem_names}

    def wait(self, eng, sem, val=None):
        tgt = self.v[sem] if val is None else val
        if tgt <= 0:
            return
        h = self.h[sem]
        self.thunks.append((eng, lambda e: e.wait_ge(h, tgt)))

    def do(self, eng, fn, inc=None):
        incs = ([inc] if inc else [])
        hs = [(self.h[s], a) for s, a in incs]
        for s_, a in incs:
            self.v[s_] += a

        def run(e, fn=fn, hs=hs):
            ins = fn(e)
            for h, a in hs:
                ins = ins.then_inc(h, a)
        self.thunks.append((eng, run))

    def val(self, sem):
        return self.v[sem]

    def flush(self, blk):
        th = self.thunks
        self.thunks = []
        for name in ENGS:
            fns = [f for e, f in th if e == name]
            if fns:
                getattr(blk, name)(lambda eng, fns=fns: [f(eng) for f in fns])


class BankGuard:

    def __init__(self, p):
        self.p = p
        self.w = {}
        self.r = {}

    def pre_write(self, eng, bank):
        if bank in self.r:
            sem, val = self.r[bank]
            self.p.wait(eng, sem, val)

    def post_write(self, bank, sem, val):
        self.w[bank] = (sem, val)

    def pre_read(self, eng, bank):
        if bank in self.w:
            sem, val = self.w[bank]
            self.p.wait(eng, sem, val)

    def post_read(self, bank, sem, val):
        self.r[bank] = (sem, val)


def ap_of(t, offset, dims):
    return AP(tensor=t.ap().tensor, offset=offset, ap=[list(d) for d in dims])


def build_core(dump=(), phase_limit=99, conv_mode='full'):
    nc = bass.Bass(trn_type="TRN2", target_bir_lowering=False)
    stack = contextlib.ExitStack()

    x_in = nc.dram_tensor("x", [B2 * S, D], F32, kind="ExternalInput").ap()
    delta = nc.dram_tensor("delta", [D, N], F32, kind="ExternalInput").ap()
    alpha = nc.dram_tensor("alpha", [D, N], F32, kind="ExternalInput").ap()
    beta = nc.dram_tensor("beta", [D, N], F32, kind="ExternalInput").ap()
    gamma = nc.dram_tensor("gamma", [D, N], F32, kind="ExternalInput").ap()
    omega = nc.dram_tensor("omega", [1, D], F32, kind="ExternalInput").ap()
    router_w = nc.dram_tensor("router_w", [E, D], F32, kind="ExternalInput").ap()
    w1 = nc.dram_tensor("w1", [E * D, HID], F32, kind="ExternalInput").ap()
    b1 = nc.dram_tensor("b1", [E, HID], F32, kind="ExternalInput").ap()
    w2 = nc.dram_tensor("w2", [E * HID, D], F32, kind="ExternalInput").ap()
    b2 = nc.dram_tensor("b2", [E, D], F32, kind="ExternalInput").ap()
    HC = host_consts()
    cst = {k: nc.dram_tensor(k, list(v.shape), I16 if v.dtype == np.int16 else F32,
                             kind="ExternalInput").ap() for k, v in HC.items()}

    out = nc.dram_tensor("out", [B2 * S, D], F32, kind="ExternalOutput").ap()
    G = nc.dram_tensor("Gstair", [32 * 256, 32], F32, kind="Internal").ap()
    x1d = [nc.dram_tensor(f"x1d{b}", [S, XROW], F32, kind="Internal").ap() for b in range(B2)]
    scT_dram = nc.dram_tensor("scTd", [16, 2048], F32, kind="Internal").ap()
    ffnd = [nc.dram_tensor(f"ffnd{b}", [S, D], F32, kind="Internal").ap() for b in range(B2)]

    dbg = {}
    for nm, shp, dt_ in [("y1", [B2 * S, D], F32), ("x1", [B2 * S, XROW], F32),
                         ("ffn", [B2 * S, D], F32), ("sc128", [128, 256], F32),
                         ("st16", [16, 16], F32), ("slot16", [16, S], I16),
                         ("gbuf", [128, 2176], F32), ("vbuf", [128, 2176], F32),
                         ("Wv", [128, 512], F32), ("lqf", [1, 8192], F32),
                         ("vstar", [16, 8], F32), ("selpos", [16, NPAD], I16),
                         ("sel", [16, S], F32)]:
        if nm in dump:
            dbg[nm] = nc.dram_tensor("dbg_" + nm, shp, dt_, kind="ExternalOutput").ap()

    p = Prog(nc, stack, [
        "s_ld", "s_x", "s_g", "s_t", "s_v", "s_s", "s_scan", "s_T", "s_Tf",
        "s_x1w", "s_sel", "s_idx", "s_gth", "s_w", "s_lqf", "s_z", "s_out",
        "s_q", "s_sc0", "s_sc1",
    ])
    bg = BankGuard(p)
    blk = stack.enter_context(nc.Block())

    pt = {}

    def sbp(name, shape, dtype=F32):
        pt[name] = stack.enter_context(nc.sbuf_tensor(name, shape, dtype))
        return pt[name]

    for nm in ["logq", "cc", "cq", "lam",
               "lq_sc", "cq_sc", "lam_sc", "ident", "iota"]:
        sbp(nm, [128, 128])
    sbp("om1", [128, 1024])
    sbp("reviota", [128, 1]); sbp("ones_row", [1, 128])
    sbp("rwT", [128, 64]); sbp("z64", [128, 64])
    sbp("bo8", [128, 16]); sbp("bo8t", [16, 128])
    sbp("lt", [128, 16])
    ctxP0 = contextlib.ExitStack()
    for nm in ["pp", "aa", "bb", "gg", "q"]:
        pt[nm] = ctxP0.enter_context(nc.sbuf_tensor(nm, [128, 128], F32))

    def nat_ap(src):
        return AP(tensor=src.tensor, offset=0, ap=[[16, 128], [2048, 8], [1, 16]])

    for nm, src in [("pp", delta), ("aa", alpha), ("bb", beta), ("gg", gamma)]:
        p.do("sync", lambda e, nm=nm, src=src: e.dma_start(
            out=pt[nm].ap().rearrange("p (a b) -> p a b", a=8), in_=nat_ap(src)),
            inc=("s_ld", 16))
    p.do("sync", lambda e: e.dma_start(out=pt["om1"][0:1, :], in_=omega[:]), inc=("s_ld", 16))
    for key, nm in [("ident", "c_ident"), ("iota", "c_iota"),
                    ("reviota", "c_reviota"), ("ones_row", "c_ones_row"),
                    ("bo8", "c_bo8"), ("bo8t", "c_bo8t")]:
        p.do("sync", lambda e, key=key, nm=nm: e.dma_start(out=pt[key][:], in_=cst[nm]),
             inc=("s_ld", 16))
    for dt_ in range(8):
        p.do("sync", lambda e, dt_=dt_: e.dma_start(
            out=pt["rwT"][:, dt_ * 8:(dt_ + 1) * 8],
            in_=AP(tensor=router_w.tensor, offset=dt_ * 128, ap=[[1, 128], [1024, 8]]),
            allow_slow_non_contiguous=True), inc=("s_ld", 16))

    NLD = p.val("s_ld")

    p.wait("vector", "s_ld", NLD)
    p.do("vector", lambda e: nc.vector.memset(pt["z64"][:], 0.0), inc=("s_v", 1))
    p.wait("scalar", "s_ld", NLD)
    p.do("scalar", lambda e: nc.scalar.activation(pt["pp"][:], pt["pp"][:], AF.Sigmoid))
    p.do("scalar", lambda e: nc.scalar.activation(pt["aa"][:], pt["aa"][:], AF.Sigmoid),
         inc=("s_s", 1))
    p.wait("vector", "s_s")
    p.do("vector", lambda e: nc.vector.tensor_mul(pt["q"][:], pt["pp"][:], pt["aa"][:]))
    p.do("vector", lambda e: nc.vector.tensor_scalar(pt["q"][:], pt["q"][:], -1.0, 1.0,
                                                     OP.mult, OP.add), inc=("s_v", 1))
    p.wait("scalar", "s_v")
    p.do("scalar", lambda e: nc.scalar.activation(pt["logq"][:], pt["q"][:], AF.Ln),
         inc=("s_s", 1))
    LOGQ_S = p.val("s_s")
    p.do("scalar", lambda e: nc.scalar.activation(pt["lam"][:], pt["logq"][:], AF.Exp,
                                                  scale=128.0))
    p.do("scalar", lambda e: nc.scalar.add(pt["om1"][0:1, :], pt["om1"][0:1, :], 1.0),
         inc=("s_s", 1))
    p.wait("vector", "s_s", LOGQ_S)
    p.do("vector", lambda e: nc.vector.tensor_mul(pt["cc"][:], pt["pp"][:], pt["bb"][:]))
    p.do("vector", lambda e: nc.vector.tensor_mul(pt["cc"][:], pt["cc"][:], pt["gg"][:]))
    p.do("vector", lambda e: nc.vector.tensor_scalar(pt["cc"][:], pt["cc"][:], SCALE, None,
                                                     OP.mult), inc=("s_v", 1))
    CC_V = p.val("s_v")
    p.do("vector", lambda e: nc.vector.tensor_mul(pt["cq"][:], pt["cc"][:], pt["q"][:]),
         inc=("s_v", 1))
    CQ_V = p.val("s_v")

    with (nc.sbuf_tensor("k_nat", [128, 1024], F32) as k_nat,
          nc.sbuf_tensor("krev", [128, 1024], F32) as krev,
          nc.sbuf_tensor("krevT", [128, 1024], F32) as krevT,
          nc.sbuf_tensor("expb", [128, 256], F32) as expb,
          nc.psum_tensor("psA", [128, 512], F32) as psA,
          nc.psum_tensor("psB", [128, 512], F32) as psB):

        p.do("vector", lambda e: nc.vector.memset(k_nat[:], 0.0), inc=("s_v", 1))
        p.wait("scalar", "s_v")
        SV0 = p.val("s_v")
        SS0 = p.val("s_s")
        for i in range(128):
            sl = i % 2
            if i >= 2:
                p.wait("scalar", "s_v", SV0 + i - 1)
            p.do("scalar", lambda e, i=i, sl=sl: nc.scalar.activation(
                expb[:, sl * 128:(sl + 1) * 128], pt["iota"][:], AF.Exp,
                scale=pt["logq"][:, i:i + 1]), inc=("s_s", 1))
        for i in range(128):
            sl = i % 2
            dt_ = i // 16
            p.wait("vector", "s_s", SS0 + i + 1)
            p.do("vector", lambda e, i=i, sl=sl, dt_=dt_: nc.vector.scalar_tensor_tensor(
                k_nat[:, dt_ * 128:(dt_ + 1) * 128],
                expb[:, sl * 128:(sl + 1) * 128], pt["cc"][:, i:i + 1],
                k_nat[:, dt_ * 128:(dt_ + 1) * 128], OP.mult, OP.add), inc=("s_v", 1))
        for dt_ in range(8):
            p.do("vector", lambda e, dt_=dt_: nc.vector.tensor_copy(
                krev[:, dt_ * 128:(dt_ + 1) * 128],
                k_nat[:, dt_ * 128:(dt_ + 1) * 128][:, ::-1]), inc=("s_v", 1))
        KREV_V = p.val("s_v")

        p.wait("tensor", "s_s")
        for hh in range(2):
            bank = "A" if hh == 0 else "B"
            bg.pre_write("tensor", bank)
            p.do("tensor", lambda e, hh=hh: nc.tensor.matmul(
                (psA if hh == 0 else psB)[:, 0:512], pt["ones_row"][:],
                pt["om1"][0:1, hh * 512:(hh + 1) * 512], start=True, stop=True),
                inc=("s_t", 1))
            bg.post_write(bank, "s_t", p.val("s_t"))
        for hh in range(2):
            bank = "A" if hh == 0 else "B"
            bg.pre_read("vector", bank)
            p.do("vector", lambda e, hh=hh: nc.vector.tensor_copy(
                pt["om1"][:, hh * 512:(hh + 1) * 512],
                (psA if hh == 0 else psB)[:, 0:512]), inc=("s_v", 1))
            bg.post_read(bank, "s_v", p.val("s_v"))
        p.wait("tensor", "s_v", KREV_V)
        for dt_ in range(8):
            bank = "A" if dt_ % 2 == 0 else "B"
            pb = psA if dt_ % 2 == 0 else psB
            bg.pre_write("tensor", bank)
            p.do("tensor", lambda e, dt_=dt_, pb=pb: nc.tensor.transpose(
                pb[:, 0:128], krev[:, dt_ * 128:(dt_ + 1) * 128], pt["ident"][:]),
                inc=("s_t", 1))
            bg.post_write(bank, "s_t", p.val("s_t"))
            bg.pre_read("vector", bank)
            p.do("vector", lambda e, dt_=dt_, pb=pb: nc.vector.tensor_copy(
                krevT[:, dt_ * 128:(dt_ + 1) * 128], pb[:, 0:128]), inc=("s_v", 1))
            bg.post_read(bank, "s_v", p.val("s_v"))
        KT_V = p.val("s_v")
        p.wait("sync", "s_v", KT_V)
        for half in range(2):
            for grp in range(16):
                gi = half * 16 + grp

                def fg(e, half=half, grp=grp, gi=gi):
                    src = AP(tensor=krevT.ap().tensor, offset=half * 512 + grp * 8,
                             ap=[[1024, 128], [128, 4], [1, 8]])
                    dst = AP(tensor=G.tensor, offset=gi * 256 * 32,
                             ap=[[32, 128], [8, 4], [1, 8]])
                    return e.dma_start(out=dst, in_=src)
                p.do("sync", fg, inc=("s_g", 16))

                def fz(e, gi=gi):
                    dst = AP(tensor=G.tensor, offset=gi * 256 * 32 + 128 * 32,
                             ap=[[32, 128], [1, 32]])
                    return e.dma_start(out=dst, in_=pt["z64"][:, 0:32])
                p.do("sync", fz, inc=("s_g", 16))
        NG = p.val("s_g")
        p.flush(blk)
    ctxP0.close()

    def ch_of(half, grp, dmod, dtq):
        dt_ = half * 4 + dtq
        dmf = grp * 8 + dmod
        return dt_ * 128 + dmf, dt_, dmf

    ctxX = contextlib.ExitStack()
    X2 = ctxX.enter_context(nc.sbuf_tensor("X2", [128, 2 * J * D], F32))

    def x2cols(ch):
        return ap_of(X2, ch, [[2 * J * D, 128], [J * D, 2], [D, J]])

    with (nc.sbuf_tensor("lqf", [1, 1024], F32) as lqf_t,
          nc.sbuf_tensor("vbuf", [128, 2 * 2176], F32) as vbuf,
          nc.sbuf_tensor("gbuf", [128, 2 * 2176], F32) as gbuf,
          nc.sbuf_tensor("lamseq", [128, 2176], F32) as lamseq,
          nc.sbuf_tensor("Wv", [128, 512], F32) as Wv,
          nc.sbuf_tensor("Up", [128, 1024], F32) as Up,
          nc.sbuf_tensor("Tsb", [128, 4096], F32) as Tsb,
          nc.psum_tensor("ps1", [128, 512], F32) as ps1,
          nc.psum_tensor("ps2", [128, 512], F32) as ps2,
          nc.psum_tensor("ps3", [128, 512], F32) as ps3,
          nc.psum_tensor("ps4", [128, 512], F32) as ps4):

        PS = {"1": ps1, "2": ps2, "3": ps3, "4": ps4}
        pt["lqf"] = lqf_t

        def upseg(uslot, dmod=None, rows=None):
            r = Up.ap().rearrange("p (a b) -> p a b", a=8)
            if rows is None:
                return r if dmod is None else r[:, dmod]
            r0, r1 = rows
            return r[r0:r1, dmod]

        for b in range(B2):
            def fx(e, b=b):
                src = AP(tensor=x_in.tensor, offset=b * S * D,
                         ap=[[D, 128], [C * D, J], [1, D]])
                return e.dma_start(
                    out=X2.ap().rearrange("p (b j d) -> p b j d", b=2, j=J)[:, b],
                    in_=src)
            p.do("sync", fx, inc=("s_x", 16))
        XT = p.val("s_x")

        TLAST = {0: 0}
        for half in range(2):
            srcs3 = [pt["logq"], pt["cq"], pt["lam"]]
            dsts3 = [pt["lq_sc"], pt["cq_sc"], pt["lam_sc"]]
            p.wait("vector", "s_v", CQ_V)
            for i3 in range(3):
                p.do("vector", lambda e, i3=i3: nc.vector.memset(dsts3[i3][:], 0.0),
                     inc=("s_v", 1))
            MS_V = p.val("s_v")
            p.wait("tensor", "s_v", MS_V)
            for i3 in range(3):
                for qq in range(4):
                    ii = i3 * 4 + qq
                    bank = "1" if ii % 2 == 0 else "2"
                    pb = PS[bank]
                    dt_ = half * 4 + qq
                    bg.pre_write("tensor", bank)
                    p.do("tensor", lambda e, i3=i3, dt_=dt_, pb=pb: nc.tensor.transpose(
                        pb[0:16, 0:128], srcs3[i3][:, dt_ * 16:(dt_ + 1) * 16],
                        pt["ident"][:]), inc=("s_t", 1))
                    bg.post_write(bank, "s_t", p.val("s_t"))
                    bg.pre_read("vector", bank)
                    p.do("vector", lambda e, i3=i3, qq=qq, pb=pb: nc.vector.tensor_copy(
                        dsts3[i3][32 * qq:32 * qq + 16, :], pb[0:16, 0:128]),
                        inc=("s_v", 1))
                    bg.post_read(bank, "s_v", p.val("s_v"))

            p.do("vector", lambda e: nc.vector.memset(lamseq[:], 0.0))
            for sl in range(2, 17):
                def fl(e, sl=sl):
                    dst = AP(tensor=lamseq.ap().tensor, offset=sl,
                             ap=[[2176, 128], [17, 128]])
                    return nc.vector.tensor_copy(dst, pt["lam_sc"][:])
                p.do("vector", fl)
            p.do("vector", lambda e: nc.vector.memset(vbuf[:], 0.0))
            p.do("vector", lambda e: nc.vector.memset(gbuf[:], 0.0), inc=("s_v", 1))
            BUFS_V = p.val("s_v")

            p.wait("scalar", "s_s", LOGQ_S)
            p.wait("tensor", "s_x", XT)
            p.wait("tensor", "s_v", BUFS_V)
            for grp in range(16):
                if grp % 2 == 0:
                    p.wait("scalar", "s_t")

                    def flq(e, half=half, grp=grp):
                        src_ = pt["logq"][(grp // 2) * 16:(grp // 2) * 16 + 16,
                                          half * 64:half * 64 + 64]\
                            .rearrange("p (a b) -> p a b", a=4)
                        dst = AP(tensor=pt["lqf"].ap().tensor, offset=0,
                                 ap=[[1024, 1], [64, 16], [16, 4], [1, 16]])
                        return e.dma_start(out=dst, in_=src_)
                    p.do("scalar", flq, inc=("s_lqf", 16))
                    NLQF = p.val("s_lqf")
                    p.wait("tensor", "s_lqf", NLQF)
                for j4 in range(4):
                    bank = "1" if j4 % 2 == 0 else "2"
                    pb = PS[bank]
                    base = (grp % 2) * 512 + j4 * 128
                    bg.pre_write("tensor", bank)
                    p.do("tensor", lambda e, pb=pb, base=base: nc.tensor.matmul(
                        pb[:, 0:128], pt["ones_row"][:],
                        pt["lqf"][0:1, base:base + 128], start=True, stop=True),
                        inc=("s_t", 1))
                    bg.post_write(bank, "s_t", p.val("s_t"))
                    p.wait("scalar", "s_t", p.val("s_t"))
                    bg.pre_read("scalar", bank)
                    p.do("scalar", lambda e, j4=j4, pb=pb: nc.scalar.activation(
                        Wv[:, j4 * 128:(j4 + 1) * 128], pb[:, 0:128], AF.Exp,
                        scale=pt["reviota"][:, 0:1]), inc=("s_s", 1))
                    bg.post_read(bank, "s_s", p.val("s_s"))
                WV_S = p.val("s_s")
                p.wait("tensor", "s_s", WV_S)
                pend = []
                for dmod in range(8):
                    bank = "3" if dmod % 2 == 0 else "4"
                    pb = PS[bank]
                    col = ((dmod // 2) % 4) * 32 + 256
                    if len(pend) >= 2:
                        dmf0, pb0, col0, bank0, tval0 = pend.pop(0)
                        p.wait("vector", "s_t", tval0)
                        bg.pre_read("vector", bank0)

                        def fva(e, dmf0=dmf0, pb0=pb0, col0=col0):
                            dst = AP(tensor=vbuf.ap().tensor, offset=dmf0 * 17 + 1,
                                     ap=[[2 * 2176, 128], [2176, 2], [1, 16]])
                            return nc.vector.tensor_copy(dst, pb0[:, col0:col0 + 32])
                        p.do("vector", fva, inc=("s_v", 1))
                        bg.post_read(bank0, "s_v", p.val("s_v"))
                    bg.pre_write("vector", bank)
                    p.do("vector", lambda e, pb=pb, col=col: nc.vector.memset(
                        pb[:, col:col + 32], 0.0), inc=("s_v", 1))
                    p.wait("tensor", "s_v", p.val("s_v"))
                    for dtq in range(4):
                        ch, dt_, dmf = ch_of(half, grp, dmod, dtq)
                        p.do("tensor", lambda e, ch=ch, dmod=dmod, dtq=dtq, pb=pb,
                             col=col: nc.tensor.matmul(
                                 pb[32 * dtq:32 * dtq + 16, col:col + 32],
                                 Wv[:, (dmod * 4 + dtq) * 16:(dmod * 4 + dtq + 1) * 16],
                                 x2cols(ch),
                                 start=True, stop=True, tile_position=(0, 32 * dtq)),
                             inc=("s_t", 1))
                    bg.post_write(bank, "s_t", p.val("s_t"))
                    pend.append((grp * 8 + dmod, pb, col, bank, p.val("s_t")))
                for dmf0, pb0, col0, bank0, tval0 in pend:
                    p.wait("vector", "s_t", tval0)
                    bg.pre_read("vector", bank0)

                    def fva(e, dmf0=dmf0, pb0=pb0, col0=col0):
                        dst = AP(tensor=vbuf.ap().tensor, offset=dmf0 * 17 + 1,
                                 ap=[[2 * 2176, 128], [2176, 2], [1, 16]])
                        return nc.vector.tensor_copy(dst, pb0[:, col0:col0 + 32])
                    p.do("vector", fva, inc=("s_v", 1))
                    bg.post_read(bank0, "s_v", p.val("s_v"))
                pend = []

            for b in range(B2):
                p.do("vector", lambda e, b=b: nc.vector.tensor_tensor_scan(
                    gbuf[:, b * 2176:(b + 1) * 2176], lamseq[:],
                    vbuf[:, b * 2176:(b + 1) * 2176], 0.0, OP.mult, OP.add),
                    inc=("s_scan", 1))
            p.wait("tensor", "s_scan")

            for grp in range(16):
                gi = half * 16 + grp
                uslot = grp % 2
                bankpair = ("3", "4") if grp % 2 == 0 else ("1", "2")
                if TLAST[0] > 0:
                    p.wait("scalar", "s_t", TLAST[0])
                p.wait("scalar", "s_g", NG)

                def fT(e, gi=gi):
                    src = AP(tensor=G.tensor, offset=gi * 256 * 32 + 127 * 32,
                             ap=[[32, 128], [-32, 128], [1, 32]])
                    dst = AP(tensor=Tsb.ap().tensor, offset=0,
                             ap=[[4096, 128], [32, 128], [1, 32]])
                    return e.dma_start(out=dst, in_=src)
                p.do("scalar", fT, inc=("s_T", 16))
                TN = p.val("s_T")

                p.wait("vector", "s_t")
                def fu1(e, grp=grp, uslot=uslot):
                    src = pt["lq_sc"][:, grp * 8:(grp + 1) * 8].unsqueeze(2)\
                        .broadcast_to([128, 8, 128])
                    return nc.vector.tensor_tensor(
                        upseg(uslot), src,
                        pt["iota"].ap().unsqueeze(1).broadcast_to([128, 8, 128]),
                        OP.mult)
                p.do("vector", fu1, inc=("s_v", 1))
                p.wait("scalar", "s_v")
                p.do("scalar", lambda e: nc.scalar.activation(
                    Up[:], Up[:], AF.Exp), inc=("s_s", 1))
                p.wait("vector", "s_s")
                def fu2(e, grp=grp, uslot=uslot):
                    src = pt["cq_sc"][:, grp * 8:(grp + 1) * 8].unsqueeze(2)\
                        .broadcast_to([128, 8, 128])
                    r = upseg(uslot)
                    return nc.vector.tensor_tensor(r, r, src, OP.mult)
                p.do("vector", fu2, inc=("s_v", 1))
                UP_V = p.val("s_v")

                p.wait("tensor", "s_v", UP_V)
                p.wait("tensor", "s_T", TN)
                pendB = []
                for dmod in range(8):
                    for dtq in range(4):
                        ii = dmod * 4 + dtq
                        ch, dt_, dmf = ch_of(half, grp, dmod, dtq)
                        bank = bankpair[ii % 2]
                        pb = PS[bank]
                        col = ((ii // 2) % 16) * 32
                        if len(pendB) >= 2:
                            ch0, bank0, pb0, col0, tval0 = pendB.pop(0)
                            p.wait("vector", "s_t", tval0)
                            bg.pre_read("vector", bank0)

                            def fev0(e, ch0=ch0, pb0=pb0, col0=col0):
                                xc = x2cols(ch0)
                                return nc.vector.scalar_tensor_tensor(
                                    xc, xc, pt["om1"][:, ch0:ch0 + 1],
                                    pb0[:, col0:col0 + 32], OP.mult, OP.add)
                            p.do("vector", fev0, inc=("s_v", 1))
                            bg.post_read(bank0, "s_v", p.val("s_v"))
                        bg.pre_write("tensor", bank)

                        def fmm(e, ch=ch, dmod=dmod, dtq=dtq, pb=pb, col=col,
                                uslot=uslot, dmf=dmf):
                            lhsT = AP(tensor=Tsb.ap().tensor,
                                      offset=dtq * 8 + dmod,
                                      ap=[[4096, 128], [32, 128]])
                            nc.tensor.matmul(
                                pb[:, col:col + 32], lhsT, x2cols(ch),
                                start=True, stop=False)
                            lhsT2 = upseg(uslot, dmod, rows=(32 * dtq, 32 * dtq + 32))
                            g3 = gbuf.ap().rearrange(
                                "p (b a s) -> p b a s", b=2, a=128)[
                                32 * dtq:32 * dtq + 32, :, dmf, 0:16]
                            return nc.tensor.matmul(
                                pb[:, col:col + 32], lhsT2, g3,
                                start=False, stop=True, tile_position=(32 * dtq, 0))
                        p.do("tensor", fmm, inc=("s_t", 1))
                        bg.post_write(bank, "s_t", p.val("s_t"))
                        pendB.append((ch, bank, pb, col, p.val("s_t")))
                TLAST[0] = p.val("s_t")
                for ch0, bank0, pb0, col0, tval0 in pendB:
                    p.wait("vector", "s_t", tval0)
                    bg.pre_read("vector", bank0)

                    def fev0(e, ch0=ch0, pb0=pb0, col0=col0):
                        xc = x2cols(ch0)
                        return nc.vector.scalar_tensor_tensor(
                            xc, xc, pt["om1"][:, ch0:ch0 + 1],
                            pb0[:, col0:col0 + 32], OP.mult, OP.add)
                    p.do("vector", fev0, inc=("s_v", 1))
                    bg.post_read(bank0, "s_v", p.val("s_v"))
                pendB = []

        CONV_V = p.val("s_v")
        if "y1" in dump:
            p.wait("sync", "s_v", CONV_V)
            for b in range(B2):
                def fdy(e, b=b):
                    dst = AP(tensor=dbg["y1"].tensor, offset=b * S * D,
                             ap=[[D, 128], [C * D, J], [1, D]])
                    return e.dma_start(
                        out=dst,
                        in_=X2.ap().rearrange("p (b j d) -> p b j d", b=2, j=J)[:, b])
                p.do("sync", fdy, inc=("s_out", 16))
        p.flush(blk)

    with (nc.sbuf_tensor("x1T", [128, 1024], F32) as x1T,
          nc.sbuf_tensor("scpad", [128, 64], F32) as scpad,
          nc.sbuf_tensor("sc8", [8, 128], F32) as sc8,
          nc.psum_tensor("pl1", [128, 512], F32) as ps1,
          nc.psum_tensor("pl2", [128, 512], F32) as ps2,
          nc.psum_tensor("pl3", [128, 512], F32) as ps3,
          nc.psum_tensor("pl4", [128, 512], F32) as ps4):
        p.wait("vector", "s_v", CONV_V)
        if "y1" in dump:
            p.wait("vector", "s_out")
        p.do("vector", lambda e: nc.vector.memset(scpad[:], 0.0), inc=("s_v", 1))
        for b in range(B2):
            for j in range(J):
                t0 = b * J * D + j * D
                xt = X2[:, t0:t0 + D]
                p.wait("vector", "s_x1w")
                p.do("vector", lambda e, xt=xt: nc.vector.reduce_sum(
                    pt["lt"][:, 0:1], xt, axis=AX), inc=("s_v", 1))
                p.wait("vector", "s_v")
                p.do("vector", lambda e: nc.vector.tensor_scalar(
                    pt["lt"][:, 0:1], pt["lt"][:, 0:1], 1.0 / D, None, OP.mult),
                    inc=("s_v", 1))
                p.wait("vector", "s_v")
                p.do("vector", lambda e, xt=xt: nc.vector.tensor_scalar(
                    xt, xt, pt["lt"][:, 0:1], None, OP.subtract), inc=("s_v", 1))
                LN_V = p.val("s_v")
                p.wait("scalar", "s_v", LN_V)
                p.do("scalar", lambda e, xt=xt: nc.scalar.activation(
                    x1T[:], xt, AF.Square), inc=("s_s", 1))
                p.wait("vector", "s_s")
                p.do("vector", lambda e: nc.vector.reduce_sum(
                    pt["lt"][:, 1:2], x1T[:], axis=AX), inc=("s_v", 1))
                p.wait("vector", "s_v")
                p.do("vector", lambda e: nc.vector.tensor_scalar(
                    pt["lt"][:, 2:3], pt["lt"][:, 1:2], 1.0 / D, EPS, OP.mult, OP.add),
                    inc=("s_v", 1))
                p.wait("scalar", "s_v")
                p.do("scalar", lambda e: nc.scalar.activation(
                    pt["lt"][:, 2:3], pt["lt"][:, 2:3], AF.Sqrt), inc=("s_s", 1))
                p.wait("vector", "s_s")
                p.do("vector", lambda e: nc.vector.reciprocal(
                    pt["lt"][:, 3:4], pt["lt"][:, 2:3]), inc=("s_v", 1))
                p.wait("vector", "s_v")
                p.do("vector", lambda e, xt=xt: nc.vector.tensor_scalar(
                    xt, xt, pt["lt"][:, 3:4], None, OP.mult), inc=("s_v", 1))
                X1_V = p.val("s_v")

                p.wait("tensor", "s_v", X1_V)
                for dt_ in range(8):
                    bank = "1" if dt_ % 2 == 0 else "2"
                    pb = {"1": ps1, "2": ps2}[bank]
                    bg.pre_write("tensor", bank)
                    p.do("tensor", lambda e, dt_=dt_, pb=pb, t0=t0: nc.tensor.transpose(
                        pb[:, 0:128], X2[:, t0 + dt_ * 128:t0 + (dt_ + 1) * 128],
                        pt["ident"][:]), inc=("s_t", 1))
                    bg.post_write(bank, "s_t", p.val("s_t"))
                    bg.pre_read("vector", bank)
                    p.do("vector", lambda e, dt_=dt_, pb=pb: nc.vector.tensor_copy(
                        x1T[:, dt_ * 128:(dt_ + 1) * 128], pb[:, 0:128]), inc=("s_v", 1))
                    bg.post_read(bank, "s_v", p.val("s_v"))
                X1T_V = p.val("s_v")
                p.wait("tensor", "s_v", X1T_V)
                bg.pre_write("tensor", "3")
                for dt_ in range(8):
                    p.do("tensor", lambda e, dt_=dt_: nc.tensor.matmul(
                        ps3[:, 256:264], x1T[:, dt_ * 128:(dt_ + 1) * 128],
                        pt["rwT"][:, dt_ * 8:(dt_ + 1) * 8],
                        start=(dt_ == 0), stop=(dt_ == 7)),
                        inc=("s_t", 1) if dt_ == 7 else None)
                bg.post_write("3", "s_t", p.val("s_t"))
                p.wait("vector", "s_t")
                bg.pre_read("vector", "3")
                p.do("vector", lambda e: nc.vector.reduce_max(
                    pt["lt"][:, 4:5], ps3[:, 256:264], axis=AX), inc=("s_v", 1))
                p.wait("vector", "s_v")
                p.do("vector", lambda e: nc.vector.tensor_scalar(
                    scpad[:, 0:8], ps3[:, 256:264], pt["lt"][:, 4:5], None,
                    OP.subtract), inc=("s_v", 1))
                bg.post_read("3", "s_v", p.val("s_v"))
                SM_V = p.val("s_v")
                p.wait("scalar", "s_v", SM_V)
                p.do("scalar", lambda e: nc.scalar.activation(
                    scpad[:, 0:8], scpad[:, 0:8], AF.Exp), inc=("s_s", 1))
                p.wait("vector", "s_s")
                p.do("vector", lambda e: nc.vector.reduce_sum(
                    pt["lt"][:, 5:6], scpad[:, 0:8], axis=AX), inc=("s_v", 1))
                p.wait("vector", "s_v")
                p.do("vector", lambda e: nc.vector.reciprocal(
                    pt["lt"][:, 6:7], pt["lt"][:, 5:6]), inc=("s_v", 1))
                p.wait("vector", "s_v")
                p.do("vector", lambda e: nc.vector.tensor_scalar(
                    scpad[:, 0:8], scpad[:, 0:8], pt["lt"][:, 6:7], None, OP.mult),
                    inc=("s_v", 1))
                SC_V2 = p.val("s_v")
                p.wait("tensor", "s_v", SC_V2)
                bg.pre_write("tensor", "4")
                p.do("tensor", lambda e, j=j: nc.tensor.transpose(
                    ps4[0:8, 0:128], scpad[:, 0:8], pt["ident"][:]), inc=("s_t", 1))
                bg.post_write("4", "s_t", p.val("s_t"))
                p.wait("vector", "s_t")
                bg.pre_read("vector", "4")
                p.do("vector", lambda e: nc.vector.tensor_copy(
                    sc8[:], ps4[0:8, 0:128]), inc=("s_v", 1))
                bg.post_read("4", "s_v", p.val("s_v"))
                SC8_V = p.val("s_v")
                p.wait("sync", "s_v", SC8_V)

                def fx1(e, b=b, j=j, t0=t0):
                    dst = AP(tensor=x1d[b].tensor, offset=j * 128 * XROW,
                             ap=[[XROW, 128], [1, D]])
                    return e.dma_start(out=dst, in_=X2[:, t0:t0 + D])
                p.do("sync", fx1, inc=("s_x1w", 16))

                def fsc(e, b=b, j=j):
                    dst = AP(tensor=x1d[b].tensor, offset=j * 128 * XROW + D,
                             ap=[[XROW, 128], [1, 64]])
                    return e.dma_start(out=dst, in_=scpad[:])
                p.do("sync", fsc, inc=("s_x1w", 16))

                def fsc2(e, b=b, j=j):
                    dst = AP(tensor=scT_dram.tensor, offset=b * 2048 + j * 128,
                             ap=[[2 * 2048, 8], [1, 128]])
                    return e.dma_start(out=dst, in_=sc8[:])
                p.do("sync", fsc2, inc=("s_x1w", 16))

            if "x1" in dump:
                p.wait("sync", "s_x1w")

                def fdx(e, b=b):
                    return e.dma_start(
                        out=AP(tensor=dbg["x1"].tensor, offset=b * S * XROW,
                               ap=[[XROW * 128, 16], [XROW, 128], [1, XROW]]),
                        in_=AP(tensor=x1d[b].tensor, offset=0,
                               ap=[[XROW * 128, 16], [XROW, 128], [1, XROW]]))
                p.do("sync", fdx, inc=("s_out", 16))
        p.flush(blk)
    ctxX.close()

    if phase_limit >= 3:
        ctx23 = contextlib.ExitStack()
        pt["idx_all"] = ctx23.enter_context(
            nc.sbuf_tensor("idx_all", [128, 16 * 33], I16))
        ctxsel = contextlib.ExitStack()
        for nm_, shp_, dt2 in [("sc128", [128, 256], F32), ("sc16", [16, S], F32),
                               ("st16", [16, 16], F32), ("selpos", [16, NPAD], I16),
                               ("slot16", [16, S], I16), ("msk", [16, S], F32),
                               ("msk2", [16, S], F32), ("csum", [16, S], F32),
                               ("ones16", [16, 8], F32), ("iota16", [16, S], I16),
                               ("scr2", [128, 1024], F32),
                               ("st16i", [16, 2], mybir.dt.int32)]:
            pt[nm_] = ctxsel.enter_context(nc.sbuf_tensor(nm_, shp_, dt2))
        ctx2 = contextlib.ExitStack()
        ps1 = ctx2.enter_context(nc.psum_tensor("pp1", [128, 512], F32))
        ps2 = ctx2.enter_context(nc.psum_tensor("pp2", [128, 512], F32))
        p.do("sync", lambda e: e.dma_start(out=pt["iota16"][:], in_=cst["c_iota16"]),
             inc=("s_sel", 16))
        p.do("vector", lambda e: nc.vector.memset(pt["ones16"][:], 1.0),
             inc=("s_v", 1))
        X1W_ALL = p.val("s_x1w")
        SC8B_V = p.val("s_x1w")

        def vs(eng="vector"):
            sem = "s_v" if eng == "vector" else "s_s"
            p.wait(eng, sem)

        def dov(fn, eng="vector"):
            sem = "s_v" if eng == "vector" else "s_s"
            p.do(eng, fn, inc=(sem, 1))
            p.wait(eng, sem)

        p.wait("gpsimd", "s_x1w", SC8B_V)
        for e_ in range(E):
            def f128(e, e_=e_):
                srcap = AP(tensor=scT_dram.tensor, offset=e_ * 2 * 2048,
                           ap=[[2048, 2], [256, 8], [1, 256]])
                dst = pt["sc128"][e_ * 16:(e_ + 1) * 16, :]
                return e.dma_start(out=dst, in_=srcap)
            p.do("gpsimd", f128, inc=("s_sel", 16))
            for b_ in range(B2):
                g_ = e_ * 2 + b_
                def f16(e, e_=e_, b_=b_, g_=g_):
                    return e.dma_start(
                        out=pt["sc16"][g_:g_ + 1, :],
                        in_=AP(tensor=scT_dram.tensor, offset=(e_ * 2 + b_) * 2048,
                               ap=[[2048, 1], [1, 2048]]))
                p.do("gpsimd", f16, inc=("s_sel", 16))
        SCL_N = p.val("s_sel")
        p.wait("vector", "s_sel", SCL_N)
        p.wait("scalar", "s_sel", SCL_N)
        p.wait("tensor", "s_sel", SCL_N)

        if "sc128" in dump:
            p.wait("sync", "s_sel", SCL_N)
            p.do("sync", lambda e: e.dma_start(out=dbg["sc128"], in_=pt["sc128"][:]),
                 inc=("s_q", 16))
            p.wait("vector", "s_q")
        dov(lambda e: nc.vector.memset(pt["st16"][0:16, 0:1], 0.0))
        dov(lambda e: nc.vector.memset(pt["st16"][0:16, 1:2], 1.0))
        for it in range(64):
            p.wait("scalar", "s_v")
            dov(lambda e: nc.scalar.activation(pt["st16"][0:16, 4:5],
                                               pt["st16"][0:16, 0:1], AF.Sqrt), "scalar")
            dov(lambda e: nc.scalar.activation(pt["st16"][0:16, 5:6],
                                               pt["st16"][0:16, 1:2], AF.Sqrt), "scalar")
            p.wait("vector", "s_s")
            dov(lambda e: nc.vector.tensor_mul(pt["st16"][0:16, 4:5],
                                               pt["st16"][0:16, 4:5],
                                               pt["st16"][0:16, 5:6]))
            dov(lambda e: nc.vector.tensor_scalar(pt["st16"][0:16, 5:6],
                                                  pt["st16"][0:16, 1:2],
                                                  9.5367431640625e-07, None, OP.mult))
            dov(lambda e: nc.vector.tensor_scalar(pt["st16i"][0:16, 0:1],
                                                  pt["st16"][0:16, 0:1], 0.0, None,
                                                  OP.is_gt))
            dov(lambda e: nc.vector.select(pt["st16"][0:16, 2:3],
                                           pt["st16i"][0:16, 0:1],
                                           pt["st16"][0:16, 4:5],
                                           pt["st16"][0:16, 5:6]))
            MID_V = p.val("s_v")
            p.wait("tensor", "s_v", MID_V)
            bg.pre_write("tensor", "1")
            p.do("tensor", lambda e: nc.tensor.matmul(
                ps1[:, 0:1], pt["bo8t"][:], pt["st16"][0:16, 2:3],
                start=True, stop=True), inc=("s_t", 1))
            bg.post_write("1", "s_t", p.val("s_t"))
            p.wait("vector", "s_t")
            bg.pre_read("vector", "1")
            dov(lambda e: nc.vector.tensor_scalar(
                pt["scr2"][:, 0:256], pt["sc128"][:], ps1[:, 0:1], None, OP.is_gt))
            bg.post_read("1", "s_v", p.val("s_v"))
            dov(lambda e: nc.vector.reduce_sum(pt["scr2"][:, 256:257], pt["scr2"][:, 0:256],
                                               axis=AX))
            CNT_V = p.val("s_v")
            p.wait("tensor", "s_v", CNT_V)
            bg.pre_write("tensor", "2")
            p.do("tensor", lambda e: nc.tensor.matmul(
                ps2[0:16, 0:1], pt["bo8"][:], pt["scr2"][:, 256:257],
                start=True, stop=True), inc=("s_t", 1))
            bg.post_write("2", "s_t", p.val("s_t"))
            p.wait("vector", "s_t")
            bg.pre_read("vector", "2")
            dov(lambda e: nc.vector.tensor_copy(pt["st16"][0:16, 3:4], ps2[0:16, 0:1]))
            bg.post_read("2", "s_v", p.val("s_v"))
            dov(lambda e: nc.vector.tensor_scalar(pt["st16i"][0:16, 1:2],
                                                  pt["st16"][0:16, 3:4], 513.0, None,
                                                  OP.is_ge))
            dov(lambda e: nc.vector.select(pt["st16"][0:16, 0:1],
                                           pt["st16i"][0:16, 1:2],
                                           pt["st16"][0:16, 2:3],
                                           pt["st16"][0:16, 0:1]))
            dov(lambda e: nc.vector.select(pt["st16"][0:16, 1:2],
                                           pt["st16i"][0:16, 1:2],
                                           pt["st16"][0:16, 1:2],
                                           pt["st16"][0:16, 2:3]))
            if it == 1 and "st16" in dump:
                ST_V = p.val("s_v")
                p.wait("sync", "s_v", ST_V)
                p.do("sync", lambda e: e.dma_start(out=dbg["st16"],
                                                   in_=pt["st16"][:]), inc=("s_q", 16))
                p.wait("vector", "s_q")
            if it >= 24 and (it % 8 == 0 or it == 63):
                HI_V = p.val("s_v")
                p.wait("tensor", "s_v", HI_V)
                bg.pre_write("tensor", "1")
                p.do("tensor", lambda e: nc.tensor.matmul(
                    ps1[:, 0:1], pt["bo8t"][:], pt["st16"][0:16, 1:2],
                    start=True, stop=True), inc=("s_t", 1))
                bg.post_write("1", "s_t", p.val("s_t"))
                p.wait("vector", "s_t")
                bg.pre_read("vector", "1")
                dov(lambda e: nc.vector.tensor_scalar(
                    pt["scr2"][:, 0:256], pt["sc128"][:], ps1[:, 0:1], None, OP.is_le))
                bg.post_read("1", "s_v", p.val("s_v"))
                dov(lambda e: nc.vector.tensor_mul(pt["scr2"][:, 0:256], pt["scr2"][:, 0:256],
                                                   pt["sc128"][:]))
                dov(lambda e: nc.vector.reduce_max(pt["scr2"][:, 256:257], pt["scr2"][:, 0:256],
                                                   axis=AX))
                PM_V = p.val("s_v")
                p.wait("tensor", "s_v", PM_V)
                bg.pre_write("tensor", "2")
                p.do("tensor", lambda e: nc.tensor.transpose(
                    ps2[0:1, 0:128], pt["scr2"][:, 256:257], pt["ident"][:]), inc=("s_t", 1))
                bg.post_write("2", "s_t", p.val("s_t"))
                p.wait("vector", "s_t")
                bg.pre_read("vector", "2")
                dov(lambda e: nc.vector.reduce_max(
                    pt["scr2"][0:1, 300:316],
                    ps2[0:1, 0:128].rearrange("p (a c) -> p a c", a=16),
                    axis=AX))
                bg.post_read("2", "s_v", p.val("s_v"))
                TM_V = p.val("s_v")
                p.wait("tensor", "s_v", TM_V)
                bg.pre_write("tensor", "2")
                p.do("tensor", lambda e: nc.tensor.transpose(
                    ps2[0:16, 0:1], pt["scr2"][0:1, 300:316], pt["ident"][0:1, 0:1]),
                    inc=("s_t", 1))
                bg.post_write("2", "s_t", p.val("s_t"))
                p.wait("vector", "s_t")
                bg.pre_read("vector", "2")
                dov(lambda e: nc.vector.tensor_copy(pt["st16"][0:16, 1:2],
                                                    ps2[0:16, 0:1]))
                bg.post_read("2", "s_v", p.val("s_v"))
        dov(lambda e: nc.vector.tensor_scalar(pt["msk"][:], pt["sc16"][:],
                                              pt["st16"][0:16, 1:2], None, OP.is_gt))
        dov(lambda e: nc.vector.reduce_sum(pt["st16"][0:16, 7:8], pt["msk"][:],
                                           axis=AX))
        dov(lambda e: nc.vector.tensor_scalar(pt["st16"][0:16, 8:9],
                                              pt["st16"][0:16, 7:8], -1.0, 513.0,
                                              OP.mult, OP.add))
        dov(lambda e: nc.vector.tensor_scalar(pt["msk2"][:], pt["sc16"][:],
                                              pt["st16"][0:16, 1:2], None,
                                              OP.is_equal))
        dov(lambda e: nc.vector.tensor_tensor_scan(
            pt["csum"][:], pt["ones16"][:, 0:1].broadcast_to([16, S]),
            pt["msk2"][:], 0.0, OP.mult, OP.add))
        dov(lambda e: nc.vector.tensor_scalar(pt["csum"][:], pt["csum"][:],
                                              pt["st16"][0:16, 8:9], None, OP.is_le))
        dov(lambda e: nc.vector.tensor_mul(pt["msk2"][:], pt["msk2"][:],
                                           pt["csum"][:]))
        dov(lambda e: nc.vector.tensor_add(pt["msk"][:], pt["msk"][:], pt["msk2"][:]))
        dov(lambda e: nc.vector.tensor_tensor_scan(
            pt["csum"][:], pt["ones16"][:, 0:1].broadcast_to([16, S]),
            pt["msk"][:], 0.0, OP.mult, OP.add))
        dov(lambda e: nc.vector.tensor_scalar(pt["csum"][:], pt["csum"][:], -1.0,
                                              None, OP.add))
        dov(lambda e: nc.vector.tensor_scalar(pt["msk2"][:], pt["csum"][:], 0.0625,
                                              -0.484375, OP.mult, OP.add))
        dov(lambda e: nc.vector.tensor_scalar(pt["msk2"][:], pt["msk2"][:],
                                              8388672.0, None, OP.add))
        dov(lambda e: nc.vector.tensor_scalar(pt["msk2"][:], pt["msk2"][:],
                                              -8388672.0, None, OP.add))
        dov(lambda e: nc.vector.scalar_tensor_tensor(pt["csum"][:], pt["msk2"][:],
                                                     -16.0, pt["csum"][:], OP.mult,
                                                     OP.add))
        dov(lambda e: nc.vector.scalar_tensor_tensor(pt["csum"][:], pt["csum"][:],
                                                     33.0, pt["msk2"][:], OP.mult,
                                                     OP.add))
        dov(lambda e: nc.vector.memset(pt["msk2"][:], -1.0))
        dov(lambda e: nc.vector.tensor_copy(pt["slot16"][:], pt["msk"][:]))
        dov(lambda e: nc.vector.select(pt["msk"][:], pt["slot16"][:], pt["csum"][:],
                                       pt["msk2"][:]))
        dov(lambda e: nc.vector.tensor_copy(pt["slot16"][:], pt["msk"][:]))
        SLOT_V = p.val("s_v")
        if "slot16" in dump:
            p.wait("sync", "s_v", SLOT_V)
            p.do("sync", lambda e: e.dma_start(out=dbg["slot16"], in_=pt["slot16"][:]),
                 inc=("s_q", 16))
            p.wait("vector", "s_q")
        if "vstar" in dump:
            p.wait("sync", "s_v", SLOT_V)
            if "sel" in dump:
                p.do("sync", lambda e: e.dma_start(out=dbg["sel"], in_=pt["msk"][:]),
                     inc=("s_q", 16))
            p.do("sync", lambda e: e.dma_start(out=dbg["vstar"][:, 0:8],
                                               in_=pt["st16"][0:16, 0:8]),
                 inc=("s_q", 16))
            p.wait("vector", "s_q")
        p.wait("gpsimd", "s_v", SLOT_V)
        p.do("gpsimd", lambda e: e.load_library(LIB_LS))
        p.do("gpsimd", lambda e: nc.gpsimd.local_scatter(
            pt["selpos"][:], pt["iota16"][:], pt["slot16"][:], 16, NPAD, S),
            inc=("s_idx", 1))
        p.wait("vector", "s_idx")
        dov(lambda e: nc.vector.memset(
            AP(tensor=pt["selpos"].ap().tensor, offset=65, ap=[[NPAD, 16], [33, 15]]),
            -1))
        SELP_V = p.val("s_v")
        if "selpos" in dump:
            p.wait("sync", "s_v", SELP_V)
            p.do("sync", lambda e: e.dma_start(out=dbg["selpos"], in_=pt["selpos"][:]),
                 inc=("s_q", 16))
        p.wait("scalar", "s_v", SELP_V)
        for e_ in range(E):
            for b_ in range(B2):
                g_ = e_ * 2 + b_
                be = b_ * 8 + e_
                for rep in range(2):
                    def fidx(e, g_=g_, be=be, rep=rep):
                        srcap = pt["selpos"][g_:g_ + 1, :].rearrange(
                            "p (a b) -> p a b", a=16)
                        dst = pt["idx_all"][rep * 16:(rep + 1) * 16,
                                            be * 33:(be + 1) * 33]
                        return e.dma_start(out=dst, in_=srcap)
                    p.do("scalar", fidx, inc=("s_idx", 16))
        NIDX = p.val("s_idx")

        dov(lambda e: nc.vector.memset(pt["scr2"][:, 0:1024], 0.0))
        Z_V = p.val("s_v")
        p.wait("gpsimd", "s_v", Z_V)
        for b_ in range(B2):
            for jt in range(J):
                def fz2(e, b_=b_, jt=jt):
                    return e.dma_start(out=ffnd[b_][jt * 128:(jt + 1) * 128, :],
                                      in_=pt["scr2"][:, 0:1024])
                p.do("gpsimd", fz2, inc=("s_z", 16))
        NZ = p.val("s_z")
        p.do("gpsimd", lambda e: e.load_library(LIB_MLP))
        p.flush(blk)
        ctx2.close()
        ctxsel.close()

    if phase_limit >= 4:
        ctx3 = contextlib.ExitStack()

        def _a3(name, shape, dtype=F32, psum=False):
            f = nc.psum_tensor if psum else nc.sbuf_tensor
            return ctx3.enter_context(f(name, shape, dtype))
        traw = _a3("traw", [128, 5 * XROW])
        tokT = _a3("tokT", [128, 5120])
        hT = _a3("hT", [128, 4 * 520])
        w1b = _a3("w1b", [128, 2 * 4096])
        w2b = _a3("w2b", [128, 2 * 4096])
        yT = _a3("yT", [128, 8 * 520])
        ygat = _a3("ygat", [128, 5 * 1024])
        gat = _a3("gat", [128, 8])
        pF1 = _a3("pF1", [128, 1024], psum=True)
        pF2 = _a3("pF2", [128, 1024], psum=True)
        pY1 = _a3("pY1", [128, 512], psum=True)
        pY2 = _a3("pY2", [128, 512], psum=True)
        pT = _a3("pT", [128, 512], psum=True)
        b1sb_t = _a3("b1sb_t", [128, 256])
        b2sb_t = _a3("b2sb_t", [128, 64])
        fin1 = _a3("fin1", [128, 2 * 1024])
        fin2 = _a3("fin2", [128, 2 * 1024])
        fscr = _a3("fscr", [128, 1024])
        if True:
            PSF = {"F1": pF1, "F2": pF2, "Y1": pY1, "Y2": pY2, "T": pT}
            OUTV = {0: 0, 1: 0}

            def emit_p4_tile(b_, jt, scat_target):
                sl = jt % 2
                p.wait("gpsimd", scat_sem[b_], scat_target)
                if OUTV[sl] > 0:
                    p.wait("gpsimd", "s_out", OUTV[sl])

                def ff1(e, b_=b_, jt=jt, sl=sl):
                    return e.dma_start(out=fin1[:, sl * 1024:(sl + 1) * 1024],
                                      in_=ffnd[b_][jt * 128:(jt + 1) * 128, :])
                p.do("gpsimd", ff1, inc=("s_gth", 16))

                def ff2(e, b_=b_, jt=jt, sl=sl):
                    srcap = AP(tensor=x1d[b_].tensor, offset=jt * 128 * XROW,
                               ap=[[XROW, 128], [1, D]])
                    return e.dma_start(out=fin2[:, sl * 1024:(sl + 1) * 1024],
                                      in_=srcap)
                p.do("gpsimd", ff2, inc=("s_gth", 16))
                NF = p.val("s_gth")
                f1 = fin1[:, sl * 1024:(sl + 1) * 1024]
                f2 = fin2[:, sl * 1024:(sl + 1) * 1024]
                p.wait("vector", "s_gth", NF)
                dov(lambda e, f1=f1, f2=f2: nc.vector.tensor_add(f1, f1, f2))
                dov(lambda e, f1=f1: nc.vector.reduce_sum(pt["lt"][:, 0:1], f1,
                                                          axis=AX))
                dov(lambda e: nc.vector.tensor_scalar(
                    pt["lt"][:, 0:1], pt["lt"][:, 0:1], 1.0 / D, None, OP.mult))
                dov(lambda e, f1=f1: nc.vector.tensor_scalar(
                    f1, f1, pt["lt"][:, 0:1], None, OP.subtract))
                LV = p.val("s_v")
                p.wait("scalar", "s_v", LV)
                p.do("scalar", lambda e, f1=f1: nc.scalar.activation(
                    fscr[:], f1, AF.Square), inc=("s_s", 1))
                p.wait("vector", "s_s")
                dov(lambda e: nc.vector.reduce_sum(pt["lt"][:, 1:2], fscr[:],
                                                   axis=AX))
                dov(lambda e: nc.vector.tensor_scalar(
                    pt["lt"][:, 2:3], pt["lt"][:, 1:2], 1.0 / D, EPS, OP.mult,
                    OP.add))
                p.wait("scalar", "s_v")
                p.do("scalar", lambda e: nc.scalar.activation(
                    pt["lt"][:, 2:3], pt["lt"][:, 2:3], AF.Sqrt), inc=("s_s", 1))
                p.wait("vector", "s_s")
                dov(lambda e: nc.vector.reciprocal(pt["lt"][:, 3:4],
                                                   pt["lt"][:, 2:3]))
                dov(lambda e, f1=f1: nc.vector.tensor_scalar(
                    f1, f1, pt["lt"][:, 3:4], None, OP.mult))
                OUT_V = p.val("s_v")
                p.wait("gpsimd", "s_v", OUT_V)

                def fout(e, b_=b_, jt=jt, f1=f1):
                    return e.dma_start(
                        out=out[b_ * S + jt * 128:b_ * S + (jt + 1) * 128, :],
                        in_=f1)
                p.do("gpsimd", fout, inc=("s_out", 16))
                OUTV[jt % 2] = p.val("s_out")
            scat_sem = {0: "s_sc0", 1: "s_sc1"}
            p.wait("gpsimd", "s_x1w", X1W_ALL)
            p.wait("gpsimd", "s_idx", NIDX)
            p.wait("gpsimd", "s_z", NZ)
            WFREE = {0: 0, 1: 0}
            pt["b1sb"], pt["b2sb"] = b1sb_t, b2sb_t
            for e2 in range(8):
                p.do("sync", lambda e, e2=e2: e.dma_start(
                    out=pt["b1sb"][:, e2 * 32:(e2 + 1) * 32],
                    in_=AP(tensor=b1.tensor, offset=e2 * HID, ap=[[1, 128], [128, 32]]),
                    allow_slow_non_contiguous=True), inc=("s_ld", 16))
                p.do("sync", lambda e, e2=e2: e.dma_start(
                    out=pt["b2sb"][:, e2 * 8:(e2 + 1) * 8],
                    in_=AP(tensor=b2.tensor, offset=e2 * D, ap=[[1, 128], [128, 8]]),
                    allow_slow_non_contiguous=True), inc=("s_ld", 16))
            p.wait("scalar", "s_ld")
            p.wait("vector", "s_ld")
            pairs = [(e_, b_) for b_ in range(B2) for e_ in range(E)]

            def emit_gather(k):
                e_, b_ = pairs[k]
                be = b_ * 8 + e_
                p.wait("gpsimd", "s_v")

                def fgth(e, b_=b_, be=be):
                    return nc.gpsimd.dma_gather(
                        traw.ap().rearrange("p (c r) -> p c r", c=5),
                        x1d[b_], pt["idx_all"][:, be * 33:(be + 1) * 33],
                        NPAD, CAP, XROW)
                p.do("gpsimd", fgth, inc=("s_gth", 16))
                return p.val("s_gth")

            NGTH = emit_gather(0)
            for kpair in range(len(pairs)):
                e_, b_ = pairs[kpair]
                if True:
                    be = b_ * 8 + e_
                    p.wait("vector", "s_gth", NGTH)
                    dov(lambda e, e_=e_: nc.vector.tensor_copy(
                        gat[:, 0:5],
                        ap_of(traw, 1024 + e_, [[5 * XROW, 128], [XROW, 5]])))
                    p.wait("tensor", "s_gth", NGTH)
                    for ct in range(5):
                        for dt_ in range(8):
                            ii = ct * 8 + dt_
                            bank = "F1" if ii % 2 == 0 else "F2"
                            pb = PSF[bank]
                            bg.pre_write("tensor", bank)
                            p.do("tensor", lambda e, ct=ct, dt_=dt_, pb=pb:
                                 nc.tensor.transpose(
                                     pb[:, 0:128],
                                     traw[:, ct * XROW + dt_ * 128:ct * XROW + (dt_ + 1) * 128],
                                     pt["ident"][:]), inc=("s_t", 1))
                            bg.post_write(bank, "s_t", p.val("s_t"))
                            p.wait("vector", "s_t")
                            bg.pre_read("vector", bank)
                            p.do("vector", lambda e, ct=ct, dt_=dt_, pb=pb:
                                 nc.vector.tensor_copy(
                                     tokT[:, dt_ * 640 + ct * 128:dt_ * 640 + (ct + 1) * 128],
                                     pb[:, 0:128]), inc=("s_v", 1))
                            bg.post_read(bank, "s_v", p.val("s_v"))
                    TOKT_V = p.val("s_v")
                    if kpair + 1 < len(pairs):
                        NGTH = emit_gather(kpair + 1)
                    for hb in range(8):
                        step = (e_ * 2 + b_) * 8 + hb
                        wslot = step % 2
                        if WFREE[wslot] > 0:
                            p.wait("sync", "s_t", WFREE[wslot])

                        def fw1(e, e_=e_, hb=hb, wslot=wslot):
                            srcap = AP(tensor=w1.tensor,
                                       offset=e_ * D * HID + hb * 512,
                                       ap=[[HID, 128], [128 * HID, 8], [1, 512]])
                            return e.dma_start(
                                out=w1b.ap().rearrange("p (w a c) -> p w a c", w=2, a=8)[:, wslot],
                                in_=srcap)
                        p.do("sync", fw1, inc=("s_w", 16))

                        def fw2(e, e_=e_, hb=hb, wslot=wslot):
                            srcap = AP(tensor=w2.tensor,
                                       offset=(e_ * HID + hb * 512) * D,
                                       ap=[[D, 128], [128 * D, 4], [1, 1024]])
                            return e.dma_start(
                                out=w2b.ap().rearrange("p (w a c) -> p w a c", w=2, a=4)[:, wslot],
                                in_=srcap)
                        p.do("sync", fw2, inc=("s_w", 16))
                        NW = p.val("s_w")
                        p.wait("tensor", "s_w", NW)
                        for ht in range(4):
                            bank = "F1" if ht % 2 == 0 else "F2"
                            pb = PSF[bank]
                            bg.pre_write("tensor", bank)
                            for nb1 in range(2):
                                c0, c1 = (0, 257) if nb1 == 0 else (257, 513)
                                pcol = 0 if nb1 == 0 else 512
                                for kdt in range(8):
                                    p.do("tensor", lambda e, kdt=kdt, ht=ht, pb=pb,
                                         wslot=wslot, c0=c0, c1=c1, pcol=pcol:
                                         nc.tensor.matmul(
                                             pb[:, pcol:pcol + c1 - c0],
                                             w1b[:, wslot * 4096 + kdt * 512 + ht * 128:
                                                 wslot * 4096 + kdt * 512 + (ht + 1) * 128],
                                             tokT[:, kdt * 640 + c0:kdt * 640 + c1],
                                             start=(kdt == 0),
                                             stop=(kdt == 7)),
                                         inc=("s_t", 1) if (kdt == 7 and nb1 == 1) else None)
                            bg.post_write(bank, "s_t", p.val("s_t"))
                            p.wait("scalar", "s_t")
                            bg.pre_read("scalar", bank)
                            p.do("scalar", lambda e, ht=ht, pb=pb, e_=e_, hb=hb:
                                 nc.scalar.activation(
                                     hT[:, ht * 520:ht * 520 + 257], pb[:, 0:257],
                                     AF.Gelu,
                                     bias=pt["b1sb"][:, e_ * 32 + hb * 4 + ht:
                                                     e_ * 32 + hb * 4 + ht + 1]))
                            p.do("scalar", lambda e, ht=ht, pb=pb, e_=e_, hb=hb:
                                 nc.scalar.activation(
                                     hT[:, ht * 520 + 257:ht * 520 + 513],
                                     pb[:, 512:768], AF.Gelu,
                                     bias=pt["b1sb"][:, e_ * 32 + hb * 4 + ht:
                                                     e_ * 32 + hb * 4 + ht + 1]),
                                 inc=("s_s", 1))
                            bg.post_read(bank, "s_s", p.val("s_s"))
                        HT_S = p.val("s_s")
                        p.wait("tensor", "s_s", HT_S)
                        for dt_ in range(8):
                            for nb in range(2):
                                n0, n1 = (0, 257) if nb == 0 else (257, 513)
                                bank = "Y1" if (dt_ * 2 + nb) % 2 == 0 else "Y2"
                                pb = PSF[bank]
                                bg.pre_write("tensor", bank)
                                for ht in range(4):
                                    p.do("tensor", lambda e, dt_=dt_, ht=ht, pb=pb,
                                         n0=n0, n1=n1, wslot=wslot: nc.tensor.matmul(
                                             pb[:, 0:n1 - n0],
                                             w2b[:, wslot * 4096 + ht * 1024 + dt_ * 128:
                                                 wslot * 4096 + ht * 1024 + (dt_ + 1) * 128],
                                             hT[:, ht * 520 + n0:ht * 520 + n1],
                                             start=(ht == 0), stop=(ht == 3)),
                                         inc=("s_t", 1) if ht == 3 else None)
                                bg.post_write(bank, "s_t", p.val("s_t"))
                                p.wait("vector", "s_t")
                                bg.pre_read("vector", bank)
                                if hb == 0:
                                    p.do("vector", lambda e, dt_=dt_, pb=pb, n0=n0,
                                         n1=n1, e_=e_: nc.vector.tensor_scalar(
                                             yT[:, dt_ * 520 + n0:dt_ * 520 + n1],
                                             pb[:, 0:n1 - n0],
                                             pt["b2sb"][:, e_ * 8 + dt_:
                                                        e_ * 8 + dt_ + 1],
                                             None, OP.add), inc=("s_v", 1))
                                else:
                                    p.do("vector", lambda e, dt_=dt_, pb=pb, n0=n0,
                                         n1=n1: nc.vector.tensor_add(
                                             yT[:, dt_ * 520 + n0:dt_ * 520 + n1],
                                             yT[:, dt_ * 520 + n0:dt_ * 520 + n1],
                                             pb[:, 0:n1 - n0]), inc=("s_v", 1))
                                bg.post_read(bank, "s_v", p.val("s_v"))
                        WFREE[wslot] = p.val("s_t")
                    YT_DONE_V = p.val("s_v")
                    YT_DONE_S = p.val("s_s")
                    p.wait("tensor", "s_v", YT_DONE_V)
                    p.wait("tensor", "s_s", YT_DONE_S)
                    for dt_ in range(8):
                        for ct in range(4):
                            ii = dt_ * 4 + ct
                            bank = "F1" if ii % 2 == 0 else "F2"
                            pb = PSF[bank]
                            bg.pre_write("tensor", bank)
                            p.do("tensor", lambda e, dt_=dt_, ct=ct, pb=pb:
                                 nc.tensor.transpose(
                                     pb[:, 0:128],
                                     yT[:, dt_ * 520 + ct * 128:dt_ * 520 + (ct + 1) * 128],
                                     pt["ident"][:]), inc=("s_t", 1))
                            bg.post_write(bank, "s_t", p.val("s_t"))
                            p.wait("vector", "s_t")
                            bg.pre_read("vector", bank)
                            p.do("vector", lambda e, dt_=dt_, ct=ct, pb=pb:
                                 nc.vector.tensor_scalar(
                                     ygat[:, ct * 1024 + dt_ * 128:ct * 1024 + (dt_ + 1) * 128],
                                     pb[:, 0:128], gat[:, ct:ct + 1], None, OP.mult),
                                 inc=("s_v", 1))
                            bg.post_read(bank, "s_v", p.val("s_v"))
                    for dt_ in range(8):
                        bank = "T"
                        pb = PSF[bank]
                        bg.pre_write("tensor", bank)
                        p.do("tensor", lambda e, dt_=dt_, pb=pb: nc.tensor.transpose(
                            pb[0:1, 0:128], yT[:, dt_ * 520 + 512:dt_ * 520 + 513],
                            pt["ident"][:]), inc=("s_t", 1))
                        bg.post_write(bank, "s_t", p.val("s_t"))
                        p.wait("vector", "s_t")
                        bg.pre_read("vector", bank)
                        p.do("vector", lambda e, dt_=dt_, pb=pb:
                             nc.vector.tensor_scalar(
                                 ygat[0:1, 4 * 1024 + dt_ * 128:4 * 1024 + (dt_ + 1) * 128],
                                 pb[0:1, 0:128], gat[0:1, 4:5], None, OP.mult),
                             inc=("s_v", 1))
                        bg.post_read(bank, "s_v", p.val("s_v"))
                    YG_V = p.val("s_v")
                    p.wait("gpsimd", "s_v", YG_V)
                    p.wait("gpsimd", scat_sem[b_])

                    def fsct(e, b_=b_, be=be):
                        return nc.gpsimd.dma_scatter_add(
                            ffnd[b_], ygat.ap().rearrange("p (c r) -> p c r", c=5),
                            pt["idx_all"][:, be * 33:(be + 1) * 33], NPAD, CAP, 1024)
                    p.do("gpsimd", fsct, inc=(scat_sem[b_], 16))
                    if kpair == 7:
                        SC0_ALL = p.val("s_sc0")
                    if kpair >= 8:
                        for jt2 in range(2 * (kpair - 8), 2 * (kpair - 8) + 2):
                            emit_p4_tile(0, jt2, SC0_ALL)
            SC1_ALL = p.val("s_sc1")
            for jt2 in range(16):
                emit_p4_tile(1, jt2, SC1_ALL)
            p.flush(blk)
        ctx3.close()

    p.wait("gpsimd", "s_out")
    p.wait("gpsimd", "s_x1w")
    p.wait("gpsimd", "s_g")
    p.wait("gpsimd", "s_q")
    p.do("gpsimd", lambda e: nc.gpsimd.memset(pt["z64"][0:1, 0:1], 0.0))
    p.flush(blk)
    if phase_limit >= 3:
        ctx23.close()
    stack.close()
    return nc, dbg



def kernel(L0_attn_x, L0_attn_delta, L0_attn_alpha, L0_attn_beta, L0_attn_gamma,
           L0_attn_omega, L0_ffn_router_w, L0_ffn_w1, L0_ffn_b1, L0_ffn_w2,
           L0_ffn_b2):
    n_cores = 8
    x = np.ascontiguousarray(np.asarray(L0_attn_x, dtype=np.float32))
    consts = host_consts()
    common = dict(
        delta=np.ascontiguousarray(np.asarray(L0_attn_delta, np.float32).reshape(D, N)),
        alpha=np.ascontiguousarray(np.asarray(L0_attn_alpha, np.float32).reshape(D, N)),
        beta=np.ascontiguousarray(np.asarray(L0_attn_beta, np.float32).reshape(D, N)),
        gamma=np.ascontiguousarray(np.asarray(L0_attn_gamma, np.float32).reshape(D, N)),
        omega=np.ascontiguousarray(np.asarray(L0_attn_omega, np.float32).reshape(1, D)),
        router_w=np.ascontiguousarray(np.asarray(L0_ffn_router_w, np.float32)),
        w1=np.ascontiguousarray(np.asarray(L0_ffn_w1, np.float32).reshape(E * D, HID)),
        b1=np.ascontiguousarray(np.asarray(L0_ffn_b1, np.float32)),
        w2=np.ascontiguousarray(np.asarray(L0_ffn_w2, np.float32).reshape(E * HID, D)),
        b2=np.ascontiguousarray(np.asarray(L0_ffn_b2, np.float32)),
        **consts,
    )
    nc, _ = build_core()
    lower_extended_insts(nc)
    in_maps = []
    for i in range(n_cores):
        m = dict(common)
        m["x"] = np.ascontiguousarray(x[i * B2:(i + 1) * B2].reshape(B2 * S, D))
        in_maps.append(m)
    res = run_bass_kernel_spmd(nc, in_maps, core_ids=list(range(n_cores)))
    out = np.concatenate(
        [r["out"].reshape(B2, S, D) for r in res.results], axis=0)
    return np.ascontiguousarray(out.astype(np.float32))


# version 11
# speedup vs baseline: 1.4463x; 1.0184x over previous
import contextlib
import numpy as np
import concourse.bass as bass
import concourse.mybir as mybir
from concourse.bass_types import AP
from concourse.library_config import mlp as LIB_MLP, local_scatter as LIB_LS
from concourse.library_overlay import lower_extended_insts
from concourse.bass_utils import run_bass_kernel_spmd

F32 = mybir.dt.float32
I16 = mybir.dt.int16
OP = mybir.AluOpType
AF = mybir.ActivationFunctionType
AX = mybir.AxisListType.X

B2 = 2
S = 2048
D = 1024
N = 16
E = 8
HID = 4096
CAP = 513
NPAD = 528
C = 128
J = S // C
XROW = 1088
SCALE = 0.25
EPS = 1e-5
ENGS = ("sync", "tensor", "vector", "scalar", "gpsimd")


def host_consts():
    ident = np.eye(128, dtype=np.float32)
    iota_mat = np.tile(np.arange(128, dtype=np.float32)[None, :], (128, 1))
    iota1_mat = np.tile(np.arange(0, 128, dtype=np.float32)[None, :], (128, 8)).reshape(128, 1024)
    reviota_col = (127.0 - np.arange(128, dtype=np.float32))[:, None].copy()
    ones_row = np.ones((1, 128), dtype=np.float32)
    iota16 = np.ascontiguousarray(np.tile(np.arange(S, dtype=np.int16)[None, :], (16, 1)))
    bo8 = np.zeros((128, 16), dtype=np.float32)
    for pp in range(128):
        bo8[pp, pp // 8] = 1.0
    bo8t = np.ascontiguousarray(bo8.T)
    ones16 = np.ones((16, S), dtype=np.float32)
    return dict(c_ident=ident, c_iota=iota_mat, c_iota1=iota1_mat,
                c_reviota=reviota_col, c_ones_row=ones_row, c_iota16=iota16,
                c_bo8=bo8, c_bo8t=bo8t, c_ones16=ones16)


class Prog:
    def __init__(self, nc, stack, sem_names):
        self.nc = nc
        self.thunks = []
        self.h = {n: stack.enter_context(nc.semaphore(n)) for n in sem_names}
        self.v = {n: 0 for n in sem_names}

    def wait(self, eng, sem, val=None):
        tgt = self.v[sem] if val is None else val
        if tgt <= 0:
            return
        h = self.h[sem]
        self.thunks.append((eng, lambda e: e.wait_ge(h, tgt)))

    def do(self, eng, fn, inc=None):
        incs = ([inc] if inc else [])
        hs = [(self.h[s], a) for s, a in incs]
        for s_, a in incs:
            self.v[s_] += a

        def run(e, fn=fn, hs=hs):
            ins = fn(e)
            for h, a in hs:
                ins = ins.then_inc(h, a)
        self.thunks.append((eng, run))

    def val(self, sem):
        return self.v[sem]

    def flush(self, blk):
        th = self.thunks
        self.thunks = []
        for name in ENGS:
            fns = [f for e, f in th if e == name]
            if fns:
                getattr(blk, name)(lambda eng, fns=fns: [f(eng) for f in fns])


class BankGuard:

    def __init__(self, p):
        self.p = p
        self.w = {}
        self.r = {}

    def pre_write(self, eng, bank):
        if bank in self.r:
            sem, val = self.r[bank]
            self.p.wait(eng, sem, val)

    def post_write(self, bank, sem, val):
        self.w[bank] = (sem, val)

    def pre_read(self, eng, bank):
        if bank in self.w:
            sem, val = self.w[bank]
            self.p.wait(eng, sem, val)

    def post_read(self, bank, sem, val):
        self.r[bank] = (sem, val)


def ap_of(t, offset, dims):
    return AP(tensor=t.ap().tensor, offset=offset, ap=[list(d) for d in dims])


def build_core(dump=(), phase_limit=99, conv_mode='full'):
    nc = bass.Bass(trn_type="TRN2", target_bir_lowering=False)
    stack = contextlib.ExitStack()

    x_in = nc.dram_tensor("x", [B2 * S, D], F32, kind="ExternalInput").ap()
    delta = nc.dram_tensor("delta", [D, N], F32, kind="ExternalInput").ap()
    alpha = nc.dram_tensor("alpha", [D, N], F32, kind="ExternalInput").ap()
    beta = nc.dram_tensor("beta", [D, N], F32, kind="ExternalInput").ap()
    gamma = nc.dram_tensor("gamma", [D, N], F32, kind="ExternalInput").ap()
    omega = nc.dram_tensor("omega", [1, D], F32, kind="ExternalInput").ap()
    router_w = nc.dram_tensor("router_w", [E, D], F32, kind="ExternalInput").ap()
    w1 = nc.dram_tensor("w1", [E * D, HID], F32, kind="ExternalInput").ap()
    b1 = nc.dram_tensor("b1", [E, HID], F32, kind="ExternalInput").ap()
    w2 = nc.dram_tensor("w2", [E * HID, D], F32, kind="ExternalInput").ap()
    b2 = nc.dram_tensor("b2", [E, D], F32, kind="ExternalInput").ap()
    HC = host_consts()
    cst = {k: nc.dram_tensor(k, list(v.shape), I16 if v.dtype == np.int16 else F32,
                             kind="ExternalInput").ap() for k, v in HC.items()}

    out = nc.dram_tensor("out", [B2 * S, D], F32, kind="ExternalOutput").ap()
    G = nc.dram_tensor("Gstair", [32 * 256, 32], F32, kind="Internal").ap()
    x1d = [nc.dram_tensor(f"x1d{b}", [S, XROW], F32, kind="Internal").ap() for b in range(B2)]
    scT_dram = nc.dram_tensor("scTd", [16, 2048], F32, kind="Internal").ap()
    ffnd = [nc.dram_tensor(f"ffnd{b}", [S, D], F32, kind="Internal").ap() for b in range(B2)]

    dbg = {}
    for nm, shp, dt_ in [("y1", [B2 * S, D], F32), ("x1", [B2 * S, XROW], F32),
                         ("ffn", [B2 * S, D], F32), ("sc128", [128, 256], F32),
                         ("st16", [16, 16], F32), ("slot16", [16, S], I16),
                         ("gbuf", [128, 2176], F32), ("vbuf", [128, 2176], F32),
                         ("Wv", [128, 512], F32), ("lqf", [1, 8192], F32),
                         ("vstar", [16, 8], F32), ("selpos", [16, NPAD], I16),
                         ("sel", [16, S], F32)]:
        if nm in dump:
            dbg[nm] = nc.dram_tensor("dbg_" + nm, shp, dt_, kind="ExternalOutput").ap()

    p = Prog(nc, stack, [
        "s_ld", "s_x", "s_g", "s_t", "s_v", "s_s", "s_scan", "s_T", "s_Tf",
        "s_x1w", "s_sel", "s_idx", "s_gth", "s_w", "s_lqf", "s_z", "s_out",
        "s_q", "s_sc0", "s_sc1",
    ])
    bg = BankGuard(p)
    blk = stack.enter_context(nc.Block())

    pt = {}

    def sbp(name, shape, dtype=F32):
        pt[name] = stack.enter_context(nc.sbuf_tensor(name, shape, dtype))
        return pt[name]

    for nm in ["logq", "cc", "cq", "lam",
               "lq_sc", "cq_sc", "lam_sc", "ident", "iota"]:
        sbp(nm, [128, 128])
    sbp("om1", [128, 1024])
    sbp("reviota", [128, 1]); sbp("ones_row", [1, 128])
    sbp("rwT", [128, 64]); sbp("z64", [128, 64])
    sbp("bo8", [128, 16]); sbp("bo8t", [16, 128])
    sbp("lt", [128, 16])
    ctxP0 = contextlib.ExitStack()
    for nm in ["pp", "aa", "bb", "gg", "q"]:
        pt[nm] = ctxP0.enter_context(nc.sbuf_tensor(nm, [128, 128], F32))

    def nat_ap(src):
        return AP(tensor=src.tensor, offset=0, ap=[[16, 128], [2048, 8], [1, 16]])

    for nm, src in [("pp", delta), ("aa", alpha), ("bb", beta), ("gg", gamma)]:
        p.do("sync", lambda e, nm=nm, src=src: e.dma_start(
            out=pt[nm].ap().rearrange("p (a b) -> p a b", a=8), in_=nat_ap(src)),
            inc=("s_ld", 16))
    p.do("sync", lambda e: e.dma_start(out=pt["om1"][0:1, :], in_=omega[:]), inc=("s_ld", 16))
    for key, nm in [("ident", "c_ident"), ("iota", "c_iota"),
                    ("reviota", "c_reviota"), ("ones_row", "c_ones_row"),
                    ("bo8", "c_bo8"), ("bo8t", "c_bo8t")]:
        p.do("sync", lambda e, key=key, nm=nm: e.dma_start(out=pt[key][:], in_=cst[nm]),
             inc=("s_ld", 16))
    for dt_ in range(8):
        p.do("sync", lambda e, dt_=dt_: e.dma_start(
            out=pt["rwT"][:, dt_ * 8:(dt_ + 1) * 8],
            in_=AP(tensor=router_w.tensor, offset=dt_ * 128, ap=[[1, 128], [1024, 8]]),
            allow_slow_non_contiguous=True), inc=("s_ld", 16))

    NLD = p.val("s_ld")

    p.wait("vector", "s_ld", NLD)
    p.do("vector", lambda e: nc.vector.memset(pt["z64"][:], 0.0), inc=("s_v", 1))
    p.wait("scalar", "s_ld", NLD)
    p.do("scalar", lambda e: nc.scalar.activation(pt["pp"][:], pt["pp"][:], AF.Sigmoid))
    p.do("scalar", lambda e: nc.scalar.activation(pt["aa"][:], pt["aa"][:], AF.Sigmoid),
         inc=("s_s", 1))
    p.wait("vector", "s_s")
    p.do("vector", lambda e: nc.vector.tensor_mul(pt["q"][:], pt["pp"][:], pt["aa"][:]))
    p.do("vector", lambda e: nc.vector.tensor_scalar(pt["q"][:], pt["q"][:], -1.0, 1.0,
                                                     OP.mult, OP.add), inc=("s_v", 1))
    p.wait("scalar", "s_v")
    p.do("scalar", lambda e: nc.scalar.activation(pt["logq"][:], pt["q"][:], AF.Ln),
         inc=("s_s", 1))
    LOGQ_S = p.val("s_s")
    p.do("scalar", lambda e: nc.scalar.activation(pt["lam"][:], pt["logq"][:], AF.Exp,
                                                  scale=128.0))
    p.do("scalar", lambda e: nc.scalar.add(pt["om1"][0:1, :], pt["om1"][0:1, :], 1.0),
         inc=("s_s", 1))
    p.wait("vector", "s_s", LOGQ_S)
    p.do("vector", lambda e: nc.vector.tensor_mul(pt["cc"][:], pt["pp"][:], pt["bb"][:]))
    p.do("vector", lambda e: nc.vector.tensor_mul(pt["cc"][:], pt["cc"][:], pt["gg"][:]))
    p.do("vector", lambda e: nc.vector.tensor_scalar(pt["cc"][:], pt["cc"][:], SCALE, None,
                                                     OP.mult), inc=("s_v", 1))
    CC_V = p.val("s_v")
    p.do("vector", lambda e: nc.vector.tensor_mul(pt["cq"][:], pt["cc"][:], pt["q"][:]),
         inc=("s_v", 1))
    CQ_V = p.val("s_v")

    with (nc.sbuf_tensor("k_nat", [128, 1024], F32) as k_nat,
          nc.sbuf_tensor("krev", [128, 1024], F32) as krev,
          nc.sbuf_tensor("krevT", [128, 1024], F32) as krevT,
          nc.sbuf_tensor("expb", [128, 256], F32) as expb,
          nc.psum_tensor("psA", [128, 512], F32) as psA,
          nc.psum_tensor("psB", [128, 512], F32) as psB):

        p.do("vector", lambda e: nc.vector.memset(k_nat[:], 0.0), inc=("s_v", 1))
        p.wait("scalar", "s_v")
        SV0 = p.val("s_v")
        SS0 = p.val("s_s")
        for i in range(128):
            sl = i % 2
            if i >= 2:
                p.wait("scalar", "s_v", SV0 + i - 1)
            p.do("scalar", lambda e, i=i, sl=sl: nc.scalar.activation(
                expb[:, sl * 128:(sl + 1) * 128], pt["iota"][:], AF.Exp,
                scale=pt["logq"][:, i:i + 1]), inc=("s_s", 1))
        for i in range(128):
            sl = i % 2
            dt_ = i // 16
            p.wait("vector", "s_s", SS0 + i + 1)
            p.do("vector", lambda e, i=i, sl=sl, dt_=dt_: nc.vector.scalar_tensor_tensor(
                k_nat[:, dt_ * 128:(dt_ + 1) * 128],
                expb[:, sl * 128:(sl + 1) * 128], pt["cc"][:, i:i + 1],
                k_nat[:, dt_ * 128:(dt_ + 1) * 128], OP.mult, OP.add), inc=("s_v", 1))
        for dt_ in range(8):
            p.do("vector", lambda e, dt_=dt_: nc.vector.tensor_copy(
                krev[:, dt_ * 128:(dt_ + 1) * 128],
                k_nat[:, dt_ * 128:(dt_ + 1) * 128][:, ::-1]), inc=("s_v", 1))
        KREV_V = p.val("s_v")

        p.wait("tensor", "s_s")
        for hh in range(2):
            bank = "A" if hh == 0 else "B"
            bg.pre_write("tensor", bank)
            p.do("tensor", lambda e, hh=hh: nc.tensor.matmul(
                (psA if hh == 0 else psB)[:, 0:512], pt["ones_row"][:],
                pt["om1"][0:1, hh * 512:(hh + 1) * 512], start=True, stop=True),
                inc=("s_t", 1))
            bg.post_write(bank, "s_t", p.val("s_t"))
        for hh in range(2):
            bank = "A" if hh == 0 else "B"
            bg.pre_read("vector", bank)
            p.do("vector", lambda e, hh=hh: nc.vector.tensor_copy(
                pt["om1"][:, hh * 512:(hh + 1) * 512],
                (psA if hh == 0 else psB)[:, 0:512]), inc=("s_v", 1))
            bg.post_read(bank, "s_v", p.val("s_v"))
        p.wait("tensor", "s_v", KREV_V)
        for dt_ in range(8):
            bank = "A" if dt_ % 2 == 0 else "B"
            pb = psA if dt_ % 2 == 0 else psB
            bg.pre_write("tensor", bank)
            p.do("tensor", lambda e, dt_=dt_, pb=pb: nc.tensor.transpose(
                pb[:, 0:128], krev[:, dt_ * 128:(dt_ + 1) * 128], pt["ident"][:]),
                inc=("s_t", 1))
            bg.post_write(bank, "s_t", p.val("s_t"))
            bg.pre_read("vector", bank)
            p.do("vector", lambda e, dt_=dt_, pb=pb: nc.vector.tensor_copy(
                krevT[:, dt_ * 128:(dt_ + 1) * 128], pb[:, 0:128]), inc=("s_v", 1))
            bg.post_read(bank, "s_v", p.val("s_v"))
        KT_V = p.val("s_v")
        p.wait("sync", "s_v", KT_V)
        for half in range(2):
            for grp in range(16):
                gi = half * 16 + grp

                def fg(e, half=half, grp=grp, gi=gi):
                    src = AP(tensor=krevT.ap().tensor, offset=half * 512 + grp * 8,
                             ap=[[1024, 128], [128, 4], [1, 8]])
                    dst = AP(tensor=G.tensor, offset=gi * 256 * 32,
                             ap=[[32, 128], [8, 4], [1, 8]])
                    return e.dma_start(out=dst, in_=src)
                p.do("sync", fg, inc=("s_g", 16))

                def fz(e, gi=gi):
                    dst = AP(tensor=G.tensor, offset=gi * 256 * 32 + 128 * 32,
                             ap=[[32, 128], [1, 32]])
                    return e.dma_start(out=dst, in_=pt["z64"][:, 0:32])
                p.do("sync", fz, inc=("s_g", 16))
        NG = p.val("s_g")
        p.flush(blk)
    ctxP0.close()

    def ch_of(half, grp, dmod, dtq):
        dt_ = half * 4 + dtq
        dmf = grp * 8 + dmod
        return dt_ * 128 + dmf, dt_, dmf

    ctxX = contextlib.ExitStack()
    X2 = ctxX.enter_context(nc.sbuf_tensor("X2", [128, 2 * J * D], F32))

    def x2cols(ch):
        return ap_of(X2, ch, [[2 * J * D, 128], [J * D, 2], [D, J]])

    with (nc.sbuf_tensor("lqf", [1, 1024], F32) as lqf_t,
          nc.sbuf_tensor("vbuf", [128, 2 * 2176], F32) as vbuf,
          nc.sbuf_tensor("gbuf", [128, 2 * 2176], F32) as gbuf,
          nc.sbuf_tensor("lamseq", [128, 2176], F32) as lamseq,
          nc.sbuf_tensor("Wv", [128, 512], F32) as Wv,
          nc.sbuf_tensor("Up", [128, 1024], F32) as Up,
          nc.sbuf_tensor("Tsb", [128, 4096], F32) as Tsb,
          nc.psum_tensor("ps1", [128, 512], F32) as ps1,
          nc.psum_tensor("ps2", [128, 512], F32) as ps2,
          nc.psum_tensor("ps3", [128, 512], F32) as ps3,
          nc.psum_tensor("ps4", [128, 512], F32) as ps4):

        PS = {"1": ps1, "2": ps2, "3": ps3, "4": ps4}
        pt["lqf"] = lqf_t

        def upseg(uslot, dmod=None, rows=None):
            r = Up.ap().rearrange("p (a b) -> p a b", a=8)
            if rows is None:
                return r if dmod is None else r[:, dmod]
            r0, r1 = rows
            return r[r0:r1, dmod]

        for b in range(B2):
            def fx(e, b=b):
                src = AP(tensor=x_in.tensor, offset=b * S * D,
                         ap=[[D, 128], [C * D, J], [1, D]])
                return e.dma_start(
                    out=X2.ap().rearrange("p (b j d) -> p b j d", b=2, j=J)[:, b],
                    in_=src)
            p.do("sync", fx, inc=("s_x", 16))
        XT = p.val("s_x")

        TLAST = {0: 0}
        for half in range(2):
            srcs3 = [pt["logq"], pt["cq"], pt["lam"]]
            dsts3 = [pt["lq_sc"], pt["cq_sc"], pt["lam_sc"]]
            p.wait("vector", "s_v", CQ_V)
            for i3 in range(3):
                p.do("vector", lambda e, i3=i3: nc.vector.memset(dsts3[i3][:], 0.0),
                     inc=("s_v", 1))
            MS_V = p.val("s_v")
            p.wait("tensor", "s_v", MS_V)
            for i3 in range(3):
                for qq in range(4):
                    ii = i3 * 4 + qq
                    bank = "1" if ii % 2 == 0 else "2"
                    pb = PS[bank]
                    dt_ = half * 4 + qq
                    bg.pre_write("tensor", bank)
                    p.do("tensor", lambda e, i3=i3, dt_=dt_, pb=pb: nc.tensor.transpose(
                        pb[0:16, 0:128], srcs3[i3][:, dt_ * 16:(dt_ + 1) * 16],
                        pt["ident"][:]), inc=("s_t", 1))
                    bg.post_write(bank, "s_t", p.val("s_t"))
                    bg.pre_read("vector", bank)
                    p.do("vector", lambda e, i3=i3, qq=qq, pb=pb: nc.vector.tensor_copy(
                        dsts3[i3][32 * qq:32 * qq + 16, :], pb[0:16, 0:128]),
                        inc=("s_v", 1))
                    bg.post_read(bank, "s_v", p.val("s_v"))

            p.do("vector", lambda e: nc.vector.memset(lamseq[:], 0.0))
            for sl in range(2, 17):
                def fl(e, sl=sl):
                    dst = AP(tensor=lamseq.ap().tensor, offset=sl,
                             ap=[[2176, 128], [17, 128]])
                    return nc.vector.tensor_copy(dst, pt["lam_sc"][:])
                p.do("vector", fl)
            p.do("vector", lambda e: nc.vector.memset(vbuf[:], 0.0))
            p.do("vector", lambda e: nc.vector.memset(gbuf[:], 0.0), inc=("s_v", 1))
            BUFS_V = p.val("s_v")

            p.wait("scalar", "s_s", LOGQ_S)
            p.wait("tensor", "s_x", XT)
            p.wait("tensor", "s_v", BUFS_V)
            for grp in range(16):
                if grp % 2 == 0:
                    p.wait("scalar", "s_t")

                    def flq(e, half=half, grp=grp):
                        src_ = pt["logq"][(grp // 2) * 16:(grp // 2) * 16 + 16,
                                          half * 64:half * 64 + 64]\
                            .rearrange("p (a b) -> p a b", a=4)
                        dst = AP(tensor=pt["lqf"].ap().tensor, offset=0,
                                 ap=[[1024, 1], [64, 16], [16, 4], [1, 16]])
                        return e.dma_start(out=dst, in_=src_)
                    p.do("scalar", flq, inc=("s_lqf", 16))
                    NLQF = p.val("s_lqf")
                    p.wait("tensor", "s_lqf", NLQF)
                for j4 in range(4):
                    bank = "1" if j4 % 2 == 0 else "2"
                    pb = PS[bank]
                    base = (grp % 2) * 512 + j4 * 128
                    bg.pre_write("tensor", bank)
                    p.do("tensor", lambda e, pb=pb, base=base: nc.tensor.matmul(
                        pb[:, 0:128], pt["ones_row"][:],
                        pt["lqf"][0:1, base:base + 128], start=True, stop=True),
                        inc=("s_t", 1))
                    bg.post_write(bank, "s_t", p.val("s_t"))
                    p.wait("scalar", "s_t", p.val("s_t"))
                    bg.pre_read("scalar", bank)
                    p.do("scalar", lambda e, j4=j4, pb=pb: nc.scalar.activation(
                        Wv[:, j4 * 128:(j4 + 1) * 128], pb[:, 0:128], AF.Exp,
                        scale=pt["reviota"][:, 0:1]), inc=("s_s", 1))
                    bg.post_read(bank, "s_s", p.val("s_s"))
                WV_S = p.val("s_s")
                p.wait("tensor", "s_s", WV_S)
                pend = []
                for dmod in range(8):
                    bank = "3" if dmod % 2 == 0 else "4"
                    pb = PS[bank]
                    col = ((dmod // 2) % 4) * 32 + 256
                    if len(pend) >= 2:
                        dmf0, pb0, col0, bank0, tval0 = pend.pop(0)
                        p.wait("vector", "s_t", tval0)
                        bg.pre_read("vector", bank0)

                        def fva(e, dmf0=dmf0, pb0=pb0, col0=col0):
                            dst = AP(tensor=vbuf.ap().tensor, offset=dmf0 * 17 + 1,
                                     ap=[[2 * 2176, 128], [2176, 2], [1, 16]])
                            return nc.vector.tensor_copy(dst, pb0[:, col0:col0 + 32])
                        p.do("vector", fva, inc=("s_v", 1))
                        bg.post_read(bank0, "s_v", p.val("s_v"))
                    bg.pre_write("vector", bank)
                    p.do("vector", lambda e, pb=pb, col=col: nc.vector.memset(
                        pb[:, col:col + 32], 0.0), inc=("s_v", 1))
                    p.wait("tensor", "s_v", p.val("s_v"))
                    for dtq in range(4):
                        ch, dt_, dmf = ch_of(half, grp, dmod, dtq)
                        p.do("tensor", lambda e, ch=ch, dmod=dmod, dtq=dtq, pb=pb,
                             col=col: nc.tensor.matmul(
                                 pb[32 * dtq:32 * dtq + 16, col:col + 32],
                                 Wv[:, (dmod * 4 + dtq) * 16:(dmod * 4 + dtq + 1) * 16],
                                 x2cols(ch),
                                 start=True, stop=True, tile_position=(0, 32 * dtq)),
                             inc=("s_t", 1))
                    bg.post_write(bank, "s_t", p.val("s_t"))
                    pend.append((grp * 8 + dmod, pb, col, bank, p.val("s_t")))
                for dmf0, pb0, col0, bank0, tval0 in pend:
                    p.wait("vector", "s_t", tval0)
                    bg.pre_read("vector", bank0)

                    def fva(e, dmf0=dmf0, pb0=pb0, col0=col0):
                        dst = AP(tensor=vbuf.ap().tensor, offset=dmf0 * 17 + 1,
                                 ap=[[2 * 2176, 128], [2176, 2], [1, 16]])
                        return nc.vector.tensor_copy(dst, pb0[:, col0:col0 + 32])
                    p.do("vector", fva, inc=("s_v", 1))
                    bg.post_read(bank0, "s_v", p.val("s_v"))
                pend = []

            for b in range(B2):
                p.do("vector", lambda e, b=b: nc.vector.tensor_tensor_scan(
                    gbuf[:, b * 2176:(b + 1) * 2176], lamseq[:],
                    vbuf[:, b * 2176:(b + 1) * 2176], 0.0, OP.mult, OP.add),
                    inc=("s_scan", 1))
            p.wait("tensor", "s_scan")

            for grp in range(16):
                gi = half * 16 + grp
                uslot = grp % 2
                bankpair = ("3", "4") if grp % 2 == 0 else ("1", "2")
                if TLAST[0] > 0:
                    p.wait("scalar", "s_t", TLAST[0])
                p.wait("scalar", "s_g", NG)

                def fT(e, gi=gi):
                    src = AP(tensor=G.tensor, offset=gi * 256 * 32 + 127 * 32,
                             ap=[[32, 128], [-32, 128], [1, 32]])
                    dst = AP(tensor=Tsb.ap().tensor, offset=0,
                             ap=[[4096, 128], [32, 128], [1, 32]])
                    return e.dma_start(out=dst, in_=src)
                p.do("scalar", fT, inc=("s_T", 16))
                TN = p.val("s_T")

                p.wait("vector", "s_t")
                def fu1(e, grp=grp, uslot=uslot):
                    src = pt["lq_sc"][:, grp * 8:(grp + 1) * 8].unsqueeze(2)\
                        .broadcast_to([128, 8, 128])
                    return nc.vector.tensor_tensor(
                        upseg(uslot), src,
                        pt["iota"].ap().unsqueeze(1).broadcast_to([128, 8, 128]),
                        OP.mult)
                p.do("vector", fu1, inc=("s_v", 1))
                p.wait("scalar", "s_v")
                p.do("scalar", lambda e: nc.scalar.activation(
                    Up[:], Up[:], AF.Exp), inc=("s_s", 1))
                p.wait("vector", "s_s")
                def fu2(e, grp=grp, uslot=uslot):
                    src = pt["cq_sc"][:, grp * 8:(grp + 1) * 8].unsqueeze(2)\
                        .broadcast_to([128, 8, 128])
                    r = upseg(uslot)
                    return nc.vector.tensor_tensor(r, r, src, OP.mult)
                p.do("vector", fu2, inc=("s_v", 1))
                UP_V = p.val("s_v")

                p.wait("tensor", "s_v", UP_V)
                p.wait("tensor", "s_T", TN)
                pendB = []
                for dmod in range(8):
                    for dtq in range(4):
                        ii = dmod * 4 + dtq
                        ch, dt_, dmf = ch_of(half, grp, dmod, dtq)
                        bank = bankpair[ii % 2]
                        pb = PS[bank]
                        col = ((ii // 2) % 16) * 32
                        if len(pendB) >= 2:
                            ch0, bank0, pb0, col0, tval0 = pendB.pop(0)
                            p.wait("vector", "s_t", tval0)
                            bg.pre_read("vector", bank0)

                            def fev0(e, ch0=ch0, pb0=pb0, col0=col0):
                                xc = x2cols(ch0)
                                return nc.vector.scalar_tensor_tensor(
                                    xc, xc, pt["om1"][:, ch0:ch0 + 1],
                                    pb0[:, col0:col0 + 32], OP.mult, OP.add)
                            p.do("vector", fev0, inc=("s_v", 1))
                            bg.post_read(bank0, "s_v", p.val("s_v"))
                        bg.pre_write("tensor", bank)

                        def fmm(e, ch=ch, dmod=dmod, dtq=dtq, pb=pb, col=col,
                                uslot=uslot, dmf=dmf):
                            lhsT = AP(tensor=Tsb.ap().tensor,
                                      offset=dtq * 8 + dmod,
                                      ap=[[4096, 128], [32, 128]])
                            nc.tensor.matmul(
                                pb[:, col:col + 32], lhsT, x2cols(ch),
                                start=True, stop=False)
                            lhsT2 = upseg(uslot, dmod, rows=(32 * dtq, 32 * dtq + 32))
                            g3 = gbuf.ap().rearrange(
                                "p (b a s) -> p b a s", b=2, a=128)[
                                32 * dtq:32 * dtq + 32, :, dmf, 0:16]
                            return nc.tensor.matmul(
                                pb[:, col:col + 32], lhsT2, g3,
                                start=False, stop=True, tile_position=(32 * dtq, 0))
                        p.do("tensor", fmm, inc=("s_t", 1))
                        bg.post_write(bank, "s_t", p.val("s_t"))
                        pendB.append((ch, bank, pb, col, p.val("s_t")))
                TLAST[0] = p.val("s_t")
                for ch0, bank0, pb0, col0, tval0 in pendB:
                    p.wait("vector", "s_t", tval0)
                    bg.pre_read("vector", bank0)

                    def fev0(e, ch0=ch0, pb0=pb0, col0=col0):
                        xc = x2cols(ch0)
                        return nc.vector.scalar_tensor_tensor(
                            xc, xc, pt["om1"][:, ch0:ch0 + 1],
                            pb0[:, col0:col0 + 32], OP.mult, OP.add)
                    p.do("vector", fev0, inc=("s_v", 1))
                    bg.post_read(bank0, "s_v", p.val("s_v"))
                pendB = []

        CONV_V = p.val("s_v")
        if "y1" in dump:
            p.wait("sync", "s_v", CONV_V)
            for b in range(B2):
                def fdy(e, b=b):
                    dst = AP(tensor=dbg["y1"].tensor, offset=b * S * D,
                             ap=[[D, 128], [C * D, J], [1, D]])
                    return e.dma_start(
                        out=dst,
                        in_=X2.ap().rearrange("p (b j d) -> p b j d", b=2, j=J)[:, b])
                p.do("sync", fdy, inc=("s_out", 16))
        p.flush(blk)

    with (nc.sbuf_tensor("x1T", [128, 1024], F32) as x1T,
          nc.sbuf_tensor("scpad", [128, 64], F32) as scpad,
          nc.sbuf_tensor("sc8", [8, 128], F32) as sc8,
          nc.psum_tensor("pl1", [128, 512], F32) as ps1,
          nc.psum_tensor("pl2", [128, 512], F32) as ps2,
          nc.psum_tensor("pl3", [128, 512], F32) as ps3,
          nc.psum_tensor("pl4", [128, 512], F32) as ps4):
        p.wait("vector", "s_v", CONV_V)
        if "y1" in dump:
            p.wait("vector", "s_out")
        p.do("vector", lambda e: nc.vector.memset(scpad[:], 0.0), inc=("s_v", 1))
        for b in range(B2):
            for j in range(J):
                t0 = b * J * D + j * D
                xt = X2[:, t0:t0 + D]
                p.wait("vector", "s_x1w")
                p.do("vector", lambda e, xt=xt: nc.vector.reduce_sum(
                    pt["lt"][:, 0:1], xt, axis=AX), inc=("s_v", 1))
                p.wait("vector", "s_v")
                p.do("vector", lambda e: nc.vector.tensor_scalar(
                    pt["lt"][:, 0:1], pt["lt"][:, 0:1], 1.0 / D, None, OP.mult),
                    inc=("s_v", 1))
                p.wait("vector", "s_v")
                p.do("vector", lambda e, xt=xt: nc.vector.tensor_scalar(
                    xt, xt, pt["lt"][:, 0:1], None, OP.subtract), inc=("s_v", 1))
                LN_V = p.val("s_v")
                p.wait("scalar", "s_v", LN_V)
                p.do("scalar", lambda e, xt=xt: nc.scalar.activation(
                    x1T[:], xt, AF.Square), inc=("s_s", 1))
                p.wait("vector", "s_s")
                p.do("vector", lambda e: nc.vector.reduce_sum(
                    pt["lt"][:, 1:2], x1T[:], axis=AX), inc=("s_v", 1))
                p.wait("vector", "s_v")
                p.do("vector", lambda e: nc.vector.tensor_scalar(
                    pt["lt"][:, 2:3], pt["lt"][:, 1:2], 1.0 / D, EPS, OP.mult, OP.add),
                    inc=("s_v", 1))
                p.wait("scalar", "s_v")
                p.do("scalar", lambda e: nc.scalar.activation(
                    pt["lt"][:, 2:3], pt["lt"][:, 2:3], AF.Sqrt), inc=("s_s", 1))
                p.wait("vector", "s_s")
                p.do("vector", lambda e: nc.vector.reciprocal(
                    pt["lt"][:, 3:4], pt["lt"][:, 2:3]), inc=("s_v", 1))
                p.wait("vector", "s_v")
                p.do("vector", lambda e, xt=xt: nc.vector.tensor_scalar(
                    xt, xt, pt["lt"][:, 3:4], None, OP.mult), inc=("s_v", 1))
                X1_V = p.val("s_v")

                p.wait("tensor", "s_v", X1_V)
                for dt_ in range(8):
                    bank = "1" if dt_ % 2 == 0 else "2"
                    pb = {"1": ps1, "2": ps2}[bank]
                    bg.pre_write("tensor", bank)
                    p.do("tensor", lambda e, dt_=dt_, pb=pb, t0=t0: nc.tensor.transpose(
                        pb[:, 0:128], X2[:, t0 + dt_ * 128:t0 + (dt_ + 1) * 128],
                        pt["ident"][:]), inc=("s_t", 1))
                    bg.post_write(bank, "s_t", p.val("s_t"))
                    bg.pre_read("vector", bank)
                    p.do("vector", lambda e, dt_=dt_, pb=pb: nc.vector.tensor_copy(
                        x1T[:, dt_ * 128:(dt_ + 1) * 128], pb[:, 0:128]), inc=("s_v", 1))
                    bg.post_read(bank, "s_v", p.val("s_v"))
                X1T_V = p.val("s_v")
                p.wait("tensor", "s_v", X1T_V)
                bg.pre_write("tensor", "3")
                for dt_ in range(8):
                    p.do("tensor", lambda e, dt_=dt_: nc.tensor.matmul(
                        ps3[:, 256:264], x1T[:, dt_ * 128:(dt_ + 1) * 128],
                        pt["rwT"][:, dt_ * 8:(dt_ + 1) * 8],
                        start=(dt_ == 0), stop=(dt_ == 7)),
                        inc=("s_t", 1) if dt_ == 7 else None)
                bg.post_write("3", "s_t", p.val("s_t"))
                p.wait("vector", "s_t")
                bg.pre_read("vector", "3")
                p.do("vector", lambda e: nc.vector.reduce_max(
                    pt["lt"][:, 4:5], ps3[:, 256:264], axis=AX), inc=("s_v", 1))
                p.wait("vector", "s_v")
                p.do("vector", lambda e: nc.vector.tensor_scalar(
                    scpad[:, 0:8], ps3[:, 256:264], pt["lt"][:, 4:5], None,
                    OP.subtract), inc=("s_v", 1))
                bg.post_read("3", "s_v", p.val("s_v"))
                SM_V = p.val("s_v")
                p.wait("scalar", "s_v", SM_V)
                p.do("scalar", lambda e: nc.scalar.activation(
                    scpad[:, 0:8], scpad[:, 0:8], AF.Exp), inc=("s_s", 1))
                p.wait("vector", "s_s")
                p.do("vector", lambda e: nc.vector.reduce_sum(
                    pt["lt"][:, 5:6], scpad[:, 0:8], axis=AX), inc=("s_v", 1))
                p.wait("vector", "s_v")
                p.do("vector", lambda e: nc.vector.reciprocal(
                    pt["lt"][:, 6:7], pt["lt"][:, 5:6]), inc=("s_v", 1))
                p.wait("vector", "s_v")
                p.do("vector", lambda e: nc.vector.tensor_scalar(
                    scpad[:, 0:8], scpad[:, 0:8], pt["lt"][:, 6:7], None, OP.mult),
                    inc=("s_v", 1))
                SC_V2 = p.val("s_v")
                p.wait("tensor", "s_v", SC_V2)
                bg.pre_write("tensor", "4")
                p.do("tensor", lambda e, j=j: nc.tensor.transpose(
                    ps4[0:8, 0:128], scpad[:, 0:8], pt["ident"][:]), inc=("s_t", 1))
                bg.post_write("4", "s_t", p.val("s_t"))
                p.wait("vector", "s_t")
                bg.pre_read("vector", "4")
                p.do("vector", lambda e: nc.vector.tensor_copy(
                    sc8[:], ps4[0:8, 0:128]), inc=("s_v", 1))
                bg.post_read("4", "s_v", p.val("s_v"))
                SC8_V = p.val("s_v")
                p.wait("sync", "s_v", SC8_V)

                def fx1(e, b=b, j=j, t0=t0):
                    dst = AP(tensor=x1d[b].tensor, offset=j * 128 * XROW,
                             ap=[[XROW, 128], [1, D]])
                    return e.dma_start(out=dst, in_=X2[:, t0:t0 + D])
                p.do("sync", fx1, inc=("s_x1w", 16))

                def fsc(e, b=b, j=j):
                    dst = AP(tensor=x1d[b].tensor, offset=j * 128 * XROW + D,
                             ap=[[XROW, 128], [1, 64]])
                    return e.dma_start(out=dst, in_=scpad[:])
                p.do("sync", fsc, inc=("s_x1w", 16))

                def fsc2(e, b=b, j=j):
                    dst = AP(tensor=scT_dram.tensor, offset=b * 2048 + j * 128,
                             ap=[[2 * 2048, 8], [1, 128]])
                    return e.dma_start(out=dst, in_=sc8[:])
                p.do("sync", fsc2, inc=("s_x1w", 16))

            if "x1" in dump:
                p.wait("sync", "s_x1w")

                def fdx(e, b=b):
                    return e.dma_start(
                        out=AP(tensor=dbg["x1"].tensor, offset=b * S * XROW,
                               ap=[[XROW * 128, 16], [XROW, 128], [1, XROW]]),
                        in_=AP(tensor=x1d[b].tensor, offset=0,
                               ap=[[XROW * 128, 16], [XROW, 128], [1, XROW]]))
                p.do("sync", fdx, inc=("s_out", 16))
        p.flush(blk)
    ctxX.close()

    if phase_limit >= 3:
        ctx23 = contextlib.ExitStack()
        pt["idx_all"] = ctx23.enter_context(
            nc.sbuf_tensor("idx_all", [128, 16 * 33], I16))
        ctxsel = contextlib.ExitStack()
        for nm_, shp_, dt2 in [("sc128", [128, 256], F32), ("sc16", [16, S], F32),
                               ("st16", [16, 16], F32), ("selpos", [16, NPAD], I16),
                               ("slot16", [16, S], I16), ("msk", [16, S], F32),
                               ("msk2", [16, S], F32), ("csum", [16, S], F32),
                               ("ones16", [16, 8], F32), ("iota16", [16, S], I16),
                               ("scr2", [128, 1024], F32),
                               ("st16i", [16, 2], mybir.dt.int32)]:
            pt[nm_] = ctxsel.enter_context(nc.sbuf_tensor(nm_, shp_, dt2))
        ctx2 = contextlib.ExitStack()
        ps1 = ctx2.enter_context(nc.psum_tensor("pp1", [128, 512], F32))
        ps2 = ctx2.enter_context(nc.psum_tensor("pp2", [128, 512], F32))
        p.do("sync", lambda e: e.dma_start(out=pt["iota16"][:], in_=cst["c_iota16"]),
             inc=("s_sel", 16))
        p.do("vector", lambda e: nc.vector.memset(pt["ones16"][:], 1.0),
             inc=("s_v", 1))
        X1W_ALL = p.val("s_x1w")
        SC8B_V = p.val("s_x1w")

        def vs(eng="vector"):
            sem = "s_v" if eng == "vector" else "s_s"
            p.wait(eng, sem)

        def dov(fn, eng="vector"):
            sem = "s_v" if eng == "vector" else "s_s"
            p.do(eng, fn, inc=(sem, 1))
            p.wait(eng, sem)

        p.wait("gpsimd", "s_x1w", SC8B_V)
        for e_ in range(E):
            def f128(e, e_=e_):
                srcap = AP(tensor=scT_dram.tensor, offset=e_ * 2 * 2048,
                           ap=[[2048, 2], [256, 8], [1, 256]])
                dst = pt["sc128"][e_ * 16:(e_ + 1) * 16, :]
                return e.dma_start(out=dst, in_=srcap)
            p.do("gpsimd", f128, inc=("s_sel", 16))
            for b_ in range(B2):
                g_ = e_ * 2 + b_
                def f16(e, e_=e_, b_=b_, g_=g_):
                    return e.dma_start(
                        out=pt["sc16"][g_:g_ + 1, :],
                        in_=AP(tensor=scT_dram.tensor, offset=(e_ * 2 + b_) * 2048,
                               ap=[[2048, 1], [1, 2048]]))
                p.do("gpsimd", f16, inc=("s_sel", 16))
        SCL_N = p.val("s_sel")
        p.wait("vector", "s_sel", SCL_N)
        p.wait("scalar", "s_sel", SCL_N)
        p.wait("tensor", "s_sel", SCL_N)

        if "sc128" in dump:
            p.wait("sync", "s_sel", SCL_N)
            p.do("sync", lambda e: e.dma_start(out=dbg["sc128"], in_=pt["sc128"][:]),
                 inc=("s_q", 16))
            p.wait("vector", "s_q")
        dov(lambda e: nc.vector.memset(pt["st16"][0:16, 0:1], 0.0))
        dov(lambda e: nc.vector.memset(pt["st16"][0:16, 1:2], 1.0))
        for it in range(64):
            p.wait("scalar", "s_v")
            dov(lambda e: nc.scalar.activation(pt["st16"][0:16, 4:5],
                                               pt["st16"][0:16, 0:1], AF.Sqrt), "scalar")
            dov(lambda e: nc.scalar.activation(pt["st16"][0:16, 5:6],
                                               pt["st16"][0:16, 1:2], AF.Sqrt), "scalar")
            p.wait("vector", "s_s")
            dov(lambda e: nc.vector.tensor_mul(pt["st16"][0:16, 4:5],
                                               pt["st16"][0:16, 4:5],
                                               pt["st16"][0:16, 5:6]))
            dov(lambda e: nc.vector.tensor_scalar(pt["st16"][0:16, 5:6],
                                                  pt["st16"][0:16, 1:2],
                                                  9.5367431640625e-07, None, OP.mult))
            dov(lambda e: nc.vector.tensor_scalar(pt["st16i"][0:16, 0:1],
                                                  pt["st16"][0:16, 0:1], 0.0, None,
                                                  OP.is_gt))
            dov(lambda e: nc.vector.select(pt["st16"][0:16, 2:3],
                                           pt["st16i"][0:16, 0:1],
                                           pt["st16"][0:16, 4:5],
                                           pt["st16"][0:16, 5:6]))
            MID_V = p.val("s_v")
            p.wait("tensor", "s_v", MID_V)
            bg.pre_write("tensor", "1")
            p.do("tensor", lambda e: nc.tensor.matmul(
                ps1[:, 0:1], pt["bo8t"][:], pt["st16"][0:16, 2:3],
                start=True, stop=True), inc=("s_t", 1))
            bg.post_write("1", "s_t", p.val("s_t"))
            p.wait("vector", "s_t")
            bg.pre_read("vector", "1")
            dov(lambda e: nc.vector.tensor_scalar(
                pt["scr2"][:, 0:256], pt["sc128"][:], ps1[:, 0:1], None, OP.is_gt))
            bg.post_read("1", "s_v", p.val("s_v"))
            dov(lambda e: nc.vector.reduce_sum(pt["scr2"][:, 256:257], pt["scr2"][:, 0:256],
                                               axis=AX))
            CNT_V = p.val("s_v")
            p.wait("tensor", "s_v", CNT_V)
            bg.pre_write("tensor", "2")
            p.do("tensor", lambda e: nc.tensor.matmul(
                ps2[0:16, 0:1], pt["bo8"][:], pt["scr2"][:, 256:257],
                start=True, stop=True), inc=("s_t", 1))
            bg.post_write("2", "s_t", p.val("s_t"))
            p.wait("vector", "s_t")
            bg.pre_read("vector", "2")
            dov(lambda e: nc.vector.tensor_copy(pt["st16"][0:16, 3:4], ps2[0:16, 0:1]))
            bg.post_read("2", "s_v", p.val("s_v"))
            dov(lambda e: nc.vector.tensor_scalar(pt["st16i"][0:16, 1:2],
                                                  pt["st16"][0:16, 3:4], 513.0, None,
                                                  OP.is_ge))
            dov(lambda e: nc.vector.select(pt["st16"][0:16, 0:1],
                                           pt["st16i"][0:16, 1:2],
                                           pt["st16"][0:16, 2:3],
                                           pt["st16"][0:16, 0:1]))
            dov(lambda e: nc.vector.select(pt["st16"][0:16, 1:2],
                                           pt["st16i"][0:16, 1:2],
                                           pt["st16"][0:16, 1:2],
                                           pt["st16"][0:16, 2:3]))
            if it == 1 and "st16" in dump:
                ST_V = p.val("s_v")
                p.wait("sync", "s_v", ST_V)
                p.do("sync", lambda e: e.dma_start(out=dbg["st16"],
                                                   in_=pt["st16"][:]), inc=("s_q", 16))
                p.wait("vector", "s_q")
            if it >= 24 and (it % 8 == 0 or it == 63):
                HI_V = p.val("s_v")
                p.wait("tensor", "s_v", HI_V)
                bg.pre_write("tensor", "1")
                p.do("tensor", lambda e: nc.tensor.matmul(
                    ps1[:, 0:1], pt["bo8t"][:], pt["st16"][0:16, 1:2],
                    start=True, stop=True), inc=("s_t", 1))
                bg.post_write("1", "s_t", p.val("s_t"))
                p.wait("vector", "s_t")
                bg.pre_read("vector", "1")
                dov(lambda e: nc.vector.tensor_scalar(
                    pt["scr2"][:, 0:256], pt["sc128"][:], ps1[:, 0:1], None, OP.is_le))
                bg.post_read("1", "s_v", p.val("s_v"))
                dov(lambda e: nc.vector.tensor_mul(pt["scr2"][:, 0:256], pt["scr2"][:, 0:256],
                                                   pt["sc128"][:]))
                dov(lambda e: nc.vector.reduce_max(pt["scr2"][:, 256:257], pt["scr2"][:, 0:256],
                                                   axis=AX))
                PM_V = p.val("s_v")
                p.wait("tensor", "s_v", PM_V)
                bg.pre_write("tensor", "2")
                p.do("tensor", lambda e: nc.tensor.transpose(
                    ps2[0:1, 0:128], pt["scr2"][:, 256:257], pt["ident"][:]), inc=("s_t", 1))
                bg.post_write("2", "s_t", p.val("s_t"))
                p.wait("vector", "s_t")
                bg.pre_read("vector", "2")
                dov(lambda e: nc.vector.reduce_max(
                    pt["scr2"][0:1, 300:316],
                    ps2[0:1, 0:128].rearrange("p (a c) -> p a c", a=16),
                    axis=AX))
                bg.post_read("2", "s_v", p.val("s_v"))
                TM_V = p.val("s_v")
                p.wait("tensor", "s_v", TM_V)
                bg.pre_write("tensor", "2")
                p.do("tensor", lambda e: nc.tensor.transpose(
                    ps2[0:16, 0:1], pt["scr2"][0:1, 300:316], pt["ident"][0:1, 0:1]),
                    inc=("s_t", 1))
                bg.post_write("2", "s_t", p.val("s_t"))
                p.wait("vector", "s_t")
                bg.pre_read("vector", "2")
                dov(lambda e: nc.vector.tensor_copy(pt["st16"][0:16, 1:2],
                                                    ps2[0:16, 0:1]))
                bg.post_read("2", "s_v", p.val("s_v"))
        dov(lambda e: nc.vector.tensor_scalar(pt["msk"][:], pt["sc16"][:],
                                              pt["st16"][0:16, 1:2], None, OP.is_gt))
        dov(lambda e: nc.vector.reduce_sum(pt["st16"][0:16, 7:8], pt["msk"][:],
                                           axis=AX))
        dov(lambda e: nc.vector.tensor_scalar(pt["st16"][0:16, 8:9],
                                              pt["st16"][0:16, 7:8], -1.0, 513.0,
                                              OP.mult, OP.add))
        dov(lambda e: nc.vector.tensor_scalar(pt["msk2"][:], pt["sc16"][:],
                                              pt["st16"][0:16, 1:2], None,
                                              OP.is_equal))
        dov(lambda e: nc.vector.tensor_tensor_scan(
            pt["csum"][:], pt["ones16"][:, 0:1].broadcast_to([16, S]),
            pt["msk2"][:], 0.0, OP.mult, OP.add))
        dov(lambda e: nc.vector.tensor_scalar(pt["csum"][:], pt["csum"][:],
                                              pt["st16"][0:16, 8:9], None, OP.is_le))
        dov(lambda e: nc.vector.tensor_mul(pt["msk2"][:], pt["msk2"][:],
                                           pt["csum"][:]))
        dov(lambda e: nc.vector.tensor_add(pt["msk"][:], pt["msk"][:], pt["msk2"][:]))
        dov(lambda e: nc.vector.tensor_tensor_scan(
            pt["csum"][:], pt["ones16"][:, 0:1].broadcast_to([16, S]),
            pt["msk"][:], 0.0, OP.mult, OP.add))
        dov(lambda e: nc.vector.tensor_scalar(pt["csum"][:], pt["csum"][:], -1.0,
                                              None, OP.add))
        dov(lambda e: nc.vector.tensor_scalar(pt["msk2"][:], pt["csum"][:], 0.0625,
                                              -0.484375, OP.mult, OP.add))
        dov(lambda e: nc.vector.tensor_scalar(pt["msk2"][:], pt["msk2"][:],
                                              8388672.0, None, OP.add))
        dov(lambda e: nc.vector.tensor_scalar(pt["msk2"][:], pt["msk2"][:],
                                              -8388672.0, None, OP.add))
        dov(lambda e: nc.vector.scalar_tensor_tensor(pt["csum"][:], pt["msk2"][:],
                                                     -16.0, pt["csum"][:], OP.mult,
                                                     OP.add))
        dov(lambda e: nc.vector.scalar_tensor_tensor(pt["csum"][:], pt["csum"][:],
                                                     33.0, pt["msk2"][:], OP.mult,
                                                     OP.add))
        dov(lambda e: nc.vector.memset(pt["msk2"][:], -1.0))
        dov(lambda e: nc.vector.tensor_copy(pt["slot16"][:], pt["msk"][:]))
        dov(lambda e: nc.vector.select(pt["msk"][:], pt["slot16"][:], pt["csum"][:],
                                       pt["msk2"][:]))
        dov(lambda e: nc.vector.tensor_copy(pt["slot16"][:], pt["msk"][:]))
        SLOT_V = p.val("s_v")
        if "slot16" in dump:
            p.wait("sync", "s_v", SLOT_V)
            p.do("sync", lambda e: e.dma_start(out=dbg["slot16"], in_=pt["slot16"][:]),
                 inc=("s_q", 16))
            p.wait("vector", "s_q")
        if "vstar" in dump:
            p.wait("sync", "s_v", SLOT_V)
            if "sel" in dump:
                p.do("sync", lambda e: e.dma_start(out=dbg["sel"], in_=pt["msk"][:]),
                     inc=("s_q", 16))
            p.do("sync", lambda e: e.dma_start(out=dbg["vstar"][:, 0:8],
                                               in_=pt["st16"][0:16, 0:8]),
                 inc=("s_q", 16))
            p.wait("vector", "s_q")
        p.wait("gpsimd", "s_v", SLOT_V)
        p.do("gpsimd", lambda e: e.load_library(LIB_LS))
        p.do("gpsimd", lambda e: nc.gpsimd.local_scatter(
            pt["selpos"][:], pt["iota16"][:], pt["slot16"][:], 16, NPAD, S),
            inc=("s_idx", 1))
        p.wait("vector", "s_idx")
        dov(lambda e: nc.vector.memset(
            AP(tensor=pt["selpos"].ap().tensor, offset=65, ap=[[NPAD, 16], [33, 15]]),
            -1))
        SELP_V = p.val("s_v")
        if "selpos" in dump:
            p.wait("sync", "s_v", SELP_V)
            p.do("sync", lambda e: e.dma_start(out=dbg["selpos"], in_=pt["selpos"][:]),
                 inc=("s_q", 16))
        p.wait("scalar", "s_v", SELP_V)
        for e_ in range(E):
            for b_ in range(B2):
                g_ = e_ * 2 + b_
                be = b_ * 8 + e_
                for rep in range(2):
                    def fidx(e, g_=g_, be=be, rep=rep):
                        srcap = pt["selpos"][g_:g_ + 1, :].rearrange(
                            "p (a b) -> p a b", a=16)
                        dst = pt["idx_all"][rep * 16:(rep + 1) * 16,
                                            be * 33:(be + 1) * 33]
                        return e.dma_start(out=dst, in_=srcap)
                    p.do("scalar", fidx, inc=("s_idx", 16))
        NIDX = p.val("s_idx")

        dov(lambda e: nc.vector.memset(pt["scr2"][:, 0:1024], 0.0))
        Z_V = p.val("s_v")
        p.wait("gpsimd", "s_v", Z_V)
        for b_ in range(B2):
            for jt in range(J):
                def fz2(e, b_=b_, jt=jt):
                    return e.dma_start(out=ffnd[b_][jt * 128:(jt + 1) * 128, :],
                                      in_=pt["scr2"][:, 0:1024])
                p.do("gpsimd", fz2, inc=("s_z", 16))
        NZ = p.val("s_z")
        p.do("gpsimd", lambda e: e.load_library(LIB_MLP))
        p.flush(blk)
        ctx2.close()
        ctxsel.close()

    if phase_limit >= 4:
        ctx3 = contextlib.ExitStack()

        def _a3(name, shape, dtype=F32, psum=False):
            f = nc.psum_tensor if psum else nc.sbuf_tensor
            return ctx3.enter_context(f(name, shape, dtype))
        traw = _a3("traw", [128, 5 * XROW])
        tokT = _a3("tokT", [128, 5120])
        hT = _a3("hT", [128, 4 * 520])
        w1b = _a3("w1b", [128, 2 * 4096])
        w2b = _a3("w2b", [128, 2 * 4096])
        yT = _a3("yT", [128, 8 * 520])
        ygat = _a3("ygat", [128, 5 * 1024])
        gat = _a3("gat", [128, 8])
        pF1 = _a3("pF1", [128, 1024], psum=True)
        pF2 = _a3("pF2", [128, 1024], psum=True)
        pY1 = _a3("pY1", [128, 512], psum=True)
        pY2 = _a3("pY2", [128, 512], psum=True)
        pT = _a3("pT", [128, 512], psum=True)
        b1sb_t = _a3("b1sb_t", [128, 256])
        b2sb_t = _a3("b2sb_t", [128, 64])
        fin1 = _a3("fin1", [128, 2 * 1024])
        fin2 = _a3("fin2", [128, 2 * 1024])
        fscr = _a3("fscr", [128, 1024])
        if True:
            PSF = {"F1": pF1, "F2": pF2, "Y1": pY1, "Y2": pY2, "T": pT}
            OUTV = {0: 0, 1: 0}

            def emit_p4_tile(b_, jt, scat_target):
                sl = jt % 2
                p.wait("sync", scat_sem[b_], scat_target)
                if OUTV[sl] > 0:
                    p.wait("sync", "s_out", OUTV[sl])

                def ff1(e, b_=b_, jt=jt, sl=sl):
                    return e.dma_start(out=fin1[:, sl * 1024:(sl + 1) * 1024],
                                      in_=ffnd[b_][jt * 128:(jt + 1) * 128, :])
                p.do("sync", ff1, inc=("s_gth", 16))

                def ff2(e, b_=b_, jt=jt, sl=sl):
                    srcap = AP(tensor=x1d[b_].tensor, offset=jt * 128 * XROW,
                               ap=[[XROW, 128], [1, D]])
                    return e.dma_start(out=fin2[:, sl * 1024:(sl + 1) * 1024],
                                      in_=srcap)
                p.do("sync", ff2, inc=("s_gth", 16))
                NF = p.val("s_gth")
                f1 = fin1[:, sl * 1024:(sl + 1) * 1024]
                f2 = fin2[:, sl * 1024:(sl + 1) * 1024]
                p.wait("vector", "s_gth", NF)
                dov(lambda e, f1=f1, f2=f2: nc.vector.tensor_add(f1, f1, f2))
                dov(lambda e, f1=f1: nc.vector.reduce_sum(pt["lt"][:, 0:1], f1,
                                                          axis=AX))
                dov(lambda e: nc.vector.tensor_scalar(
                    pt["lt"][:, 0:1], pt["lt"][:, 0:1], 1.0 / D, None, OP.mult))
                dov(lambda e, f1=f1: nc.vector.tensor_scalar(
                    f1, f1, pt["lt"][:, 0:1], None, OP.subtract))
                LV = p.val("s_v")
                p.wait("scalar", "s_v", LV)
                p.do("scalar", lambda e, f1=f1: nc.scalar.activation(
                    fscr[:], f1, AF.Square), inc=("s_s", 1))
                p.wait("vector", "s_s")
                dov(lambda e: nc.vector.reduce_sum(pt["lt"][:, 1:2], fscr[:],
                                                   axis=AX))
                dov(lambda e: nc.vector.tensor_scalar(
                    pt["lt"][:, 2:3], pt["lt"][:, 1:2], 1.0 / D, EPS, OP.mult,
                    OP.add))
                p.wait("scalar", "s_v")
                p.do("scalar", lambda e: nc.scalar.activation(
                    pt["lt"][:, 2:3], pt["lt"][:, 2:3], AF.Sqrt), inc=("s_s", 1))
                p.wait("vector", "s_s")
                dov(lambda e: nc.vector.reciprocal(pt["lt"][:, 3:4],
                                                   pt["lt"][:, 2:3]))
                dov(lambda e, f1=f1: nc.vector.tensor_scalar(
                    f1, f1, pt["lt"][:, 3:4], None, OP.mult))
                OUT_V = p.val("s_v")
                p.wait("sync", "s_v", OUT_V)

                def fout(e, b_=b_, jt=jt, f1=f1):
                    return e.dma_start(
                        out=out[b_ * S + jt * 128:b_ * S + (jt + 1) * 128, :],
                        in_=f1)
                p.do("sync", fout, inc=("s_out", 16))
                OUTV[jt % 2] = p.val("s_out")
            scat_sem = {0: "s_sc0", 1: "s_sc1"}
            p.wait("gpsimd", "s_x1w", X1W_ALL)
            p.wait("gpsimd", "s_idx", NIDX)
            p.wait("gpsimd", "s_z", NZ)
            WFREE = {0: 0, 1: 0}
            pt["b1sb"], pt["b2sb"] = b1sb_t, b2sb_t
            for e2 in range(8):
                p.do("sync", lambda e, e2=e2: e.dma_start(
                    out=pt["b1sb"][:, e2 * 32:(e2 + 1) * 32],
                    in_=AP(tensor=b1.tensor, offset=e2 * HID, ap=[[1, 128], [128, 32]]),
                    allow_slow_non_contiguous=True), inc=("s_ld", 16))
                p.do("sync", lambda e, e2=e2: e.dma_start(
                    out=pt["b2sb"][:, e2 * 8:(e2 + 1) * 8],
                    in_=AP(tensor=b2.tensor, offset=e2 * D, ap=[[1, 128], [128, 8]]),
                    allow_slow_non_contiguous=True), inc=("s_ld", 16))
            p.wait("scalar", "s_ld")
            p.wait("vector", "s_ld")
            pairs = [(e_, b_) for b_ in range(B2) for e_ in range(E)]

            def emit_gather(k):
                e_, b_ = pairs[k]
                be = b_ * 8 + e_
                p.wait("gpsimd", "s_v")

                def fgth(e, b_=b_, be=be):
                    return nc.gpsimd.dma_gather(
                        traw.ap().rearrange("p (c r) -> p c r", c=5),
                        x1d[b_], pt["idx_all"][:, be * 33:(be + 1) * 33],
                        NPAD, CAP, XROW)
                p.do("gpsimd", fgth, inc=("s_gth", 16))
                return p.val("s_gth")

            NGTH = emit_gather(0)
            for kpair in range(len(pairs)):
                e_, b_ = pairs[kpair]
                if True:
                    be = b_ * 8 + e_
                    p.wait("vector", "s_gth", NGTH)
                    dov(lambda e, e_=e_: nc.vector.tensor_copy(
                        gat[:, 0:5],
                        ap_of(traw, 1024 + e_, [[5 * XROW, 128], [XROW, 5]])))
                    p.wait("tensor", "s_gth", NGTH)
                    for ct in range(5):
                        for dt_ in range(8):
                            ii = ct * 8 + dt_
                            bank = "F1" if ii % 2 == 0 else "F2"
                            pb = PSF[bank]
                            bg.pre_write("tensor", bank)
                            p.do("tensor", lambda e, ct=ct, dt_=dt_, pb=pb:
                                 nc.tensor.transpose(
                                     pb[:, 0:128],
                                     traw[:, ct * XROW + dt_ * 128:ct * XROW + (dt_ + 1) * 128],
                                     pt["ident"][:]), inc=("s_t", 1))
                            bg.post_write(bank, "s_t", p.val("s_t"))
                            p.wait("vector", "s_t")
                            bg.pre_read("vector", bank)
                            p.do("vector", lambda e, ct=ct, dt_=dt_, pb=pb:
                                 nc.vector.tensor_copy(
                                     tokT[:, dt_ * 640 + ct * 128:dt_ * 640 + (ct + 1) * 128],
                                     pb[:, 0:128]), inc=("s_v", 1))
                            bg.post_read(bank, "s_v", p.val("s_v"))
                    TOKT_V = p.val("s_v")
                    if kpair + 1 < len(pairs):
                        NGTH = emit_gather(kpair + 1)
                    for hb in range(8):
                        step = (e_ * 2 + b_) * 8 + hb
                        wslot = step % 2
                        if WFREE[wslot] > 0:
                            p.wait("sync", "s_t", WFREE[wslot])
                            p.wait("scalar", "s_t", WFREE[wslot])

                        def fw1(e, e_=e_, hb=hb, wslot=wslot):
                            srcap = AP(tensor=w1.tensor,
                                       offset=e_ * D * HID + hb * 512,
                                       ap=[[HID, 128], [128 * HID, 8], [1, 512]])
                            return e.dma_start(
                                out=w1b.ap().rearrange("p (w a c) -> p w a c", w=2, a=8)[:, wslot],
                                in_=srcap)
                        p.do("sync", fw1, inc=("s_w", 16))

                        def fw2(e, e_=e_, hb=hb, wslot=wslot):
                            srcap = AP(tensor=w2.tensor,
                                       offset=(e_ * HID + hb * 512) * D,
                                       ap=[[D, 128], [128 * D, 4], [1, 1024]])
                            return e.dma_start(
                                out=w2b.ap().rearrange("p (w a c) -> p w a c", w=2, a=4)[:, wslot],
                                in_=srcap)
                        p.do("scalar", fw2, inc=("s_w", 16))
                        NW = p.val("s_w")
                        p.wait("tensor", "s_w", NW)
                        for ht in range(4):
                            bank = "F1" if ht % 2 == 0 else "F2"
                            pb = PSF[bank]
                            bg.pre_write("tensor", bank)
                            for nb1 in range(2):
                                c0, c1 = (0, 257) if nb1 == 0 else (257, 513)
                                pcol = 0 if nb1 == 0 else 512
                                for kdt in range(8):
                                    p.do("tensor", lambda e, kdt=kdt, ht=ht, pb=pb,
                                         wslot=wslot, c0=c0, c1=c1, pcol=pcol:
                                         nc.tensor.matmul(
                                             pb[:, pcol:pcol + c1 - c0],
                                             w1b[:, wslot * 4096 + kdt * 512 + ht * 128:
                                                 wslot * 4096 + kdt * 512 + (ht + 1) * 128],
                                             tokT[:, kdt * 640 + c0:kdt * 640 + c1],
                                             start=(kdt == 0),
                                             stop=(kdt == 7)),
                                         inc=("s_t", 1) if (kdt == 7 and nb1 == 1) else None)
                            bg.post_write(bank, "s_t", p.val("s_t"))
                            p.wait("scalar", "s_t")
                            bg.pre_read("scalar", bank)
                            p.do("scalar", lambda e, ht=ht, pb=pb, e_=e_, hb=hb:
                                 nc.scalar.activation(
                                     hT[:, ht * 520:ht * 520 + 257], pb[:, 0:257],
                                     AF.Gelu,
                                     bias=pt["b1sb"][:, e_ * 32 + hb * 4 + ht:
                                                     e_ * 32 + hb * 4 + ht + 1]))
                            p.do("scalar", lambda e, ht=ht, pb=pb, e_=e_, hb=hb:
                                 nc.scalar.activation(
                                     hT[:, ht * 520 + 257:ht * 520 + 513],
                                     pb[:, 512:768], AF.Gelu,
                                     bias=pt["b1sb"][:, e_ * 32 + hb * 4 + ht:
                                                     e_ * 32 + hb * 4 + ht + 1]),
                                 inc=("s_s", 1))
                            bg.post_read(bank, "s_s", p.val("s_s"))
                        HT_S = p.val("s_s")
                        p.wait("tensor", "s_s", HT_S)
                        for dt_ in range(8):
                            for nb in range(2):
                                n0, n1 = (0, 257) if nb == 0 else (257, 513)
                                bank = "Y1" if (dt_ * 2 + nb) % 2 == 0 else "Y2"
                                pb = PSF[bank]
                                bg.pre_write("tensor", bank)
                                for ht in range(4):
                                    p.do("tensor", lambda e, dt_=dt_, ht=ht, pb=pb,
                                         n0=n0, n1=n1, wslot=wslot: nc.tensor.matmul(
                                             pb[:, 0:n1 - n0],
                                             w2b[:, wslot * 4096 + ht * 1024 + dt_ * 128:
                                                 wslot * 4096 + ht * 1024 + (dt_ + 1) * 128],
                                             hT[:, ht * 520 + n0:ht * 520 + n1],
                                             start=(ht == 0), stop=(ht == 3)),
                                         inc=("s_t", 1) if ht == 3 else None)
                                bg.post_write(bank, "s_t", p.val("s_t"))
                                p.wait("vector", "s_t")
                                bg.pre_read("vector", bank)
                                if hb == 0:
                                    p.do("vector", lambda e, dt_=dt_, pb=pb, n0=n0,
                                         n1=n1, e_=e_: nc.vector.tensor_scalar(
                                             yT[:, dt_ * 520 + n0:dt_ * 520 + n1],
                                             pb[:, 0:n1 - n0],
                                             pt["b2sb"][:, e_ * 8 + dt_:
                                                        e_ * 8 + dt_ + 1],
                                             None, OP.add), inc=("s_v", 1))
                                else:
                                    p.do("vector", lambda e, dt_=dt_, pb=pb, n0=n0,
                                         n1=n1: nc.vector.tensor_add(
                                             yT[:, dt_ * 520 + n0:dt_ * 520 + n1],
                                             yT[:, dt_ * 520 + n0:dt_ * 520 + n1],
                                             pb[:, 0:n1 - n0]), inc=("s_v", 1))
                                bg.post_read(bank, "s_v", p.val("s_v"))
                        WFREE[wslot] = p.val("s_t")
                    YT_DONE_V = p.val("s_v")
                    YT_DONE_S = p.val("s_s")
                    p.wait("tensor", "s_v", YT_DONE_V)
                    p.wait("tensor", "s_s", YT_DONE_S)
                    for dt_ in range(8):
                        for ct in range(4):
                            ii = dt_ * 4 + ct
                            bank = "F1" if ii % 2 == 0 else "F2"
                            pb = PSF[bank]
                            bg.pre_write("tensor", bank)
                            p.do("tensor", lambda e, dt_=dt_, ct=ct, pb=pb:
                                 nc.tensor.transpose(
                                     pb[:, 0:128],
                                     yT[:, dt_ * 520 + ct * 128:dt_ * 520 + (ct + 1) * 128],
                                     pt["ident"][:]), inc=("s_t", 1))
                            bg.post_write(bank, "s_t", p.val("s_t"))
                            p.wait("vector", "s_t")
                            bg.pre_read("vector", bank)
                            p.do("vector", lambda e, dt_=dt_, ct=ct, pb=pb:
                                 nc.vector.tensor_scalar(
                                     ygat[:, ct * 1024 + dt_ * 128:ct * 1024 + (dt_ + 1) * 128],
                                     pb[:, 0:128], gat[:, ct:ct + 1], None, OP.mult),
                                 inc=("s_v", 1))
                            bg.post_read(bank, "s_v", p.val("s_v"))
                    for dt_ in range(8):
                        bank = "T"
                        pb = PSF[bank]
                        bg.pre_write("tensor", bank)
                        p.do("tensor", lambda e, dt_=dt_, pb=pb: nc.tensor.transpose(
                            pb[0:1, 0:128], yT[:, dt_ * 520 + 512:dt_ * 520 + 513],
                            pt["ident"][:]), inc=("s_t", 1))
                        bg.post_write(bank, "s_t", p.val("s_t"))
                        p.wait("vector", "s_t")
                        bg.pre_read("vector", bank)
                        p.do("vector", lambda e, dt_=dt_, pb=pb:
                             nc.vector.tensor_scalar(
                                 ygat[0:1, 4 * 1024 + dt_ * 128:4 * 1024 + (dt_ + 1) * 128],
                                 pb[0:1, 0:128], gat[0:1, 4:5], None, OP.mult),
                             inc=("s_v", 1))
                        bg.post_read(bank, "s_v", p.val("s_v"))
                    YG_V = p.val("s_v")
                    p.wait("gpsimd", "s_v", YG_V)
                    p.wait("gpsimd", scat_sem[b_])

                    def fsct(e, b_=b_, be=be):
                        return nc.gpsimd.dma_scatter_add(
                            ffnd[b_], ygat.ap().rearrange("p (c r) -> p c r", c=5),
                            pt["idx_all"][:, be * 33:(be + 1) * 33], NPAD, CAP, 1024)
                    p.do("gpsimd", fsct, inc=(scat_sem[b_], 16))
                    if kpair == 7:
                        SC0_ALL = p.val("s_sc0")
                    if kpair >= 8:
                        for jt2 in range(2 * (kpair - 8), 2 * (kpair - 8) + 2):
                            emit_p4_tile(0, jt2, SC0_ALL)
            SC1_ALL = p.val("s_sc1")
            for jt2 in range(16):
                emit_p4_tile(1, jt2, SC1_ALL)
            p.flush(blk)
        ctx3.close()

    p.wait("gpsimd", "s_out")
    p.wait("gpsimd", "s_x1w")
    p.wait("gpsimd", "s_g")
    p.wait("gpsimd", "s_q")
    p.do("gpsimd", lambda e: nc.gpsimd.memset(pt["z64"][0:1, 0:1], 0.0))
    p.flush(blk)
    if phase_limit >= 3:
        ctx23.close()
    stack.close()
    return nc, dbg



def kernel(L0_attn_x, L0_attn_delta, L0_attn_alpha, L0_attn_beta, L0_attn_gamma,
           L0_attn_omega, L0_ffn_router_w, L0_ffn_w1, L0_ffn_b1, L0_ffn_w2,
           L0_ffn_b2):
    n_cores = 8
    x = np.ascontiguousarray(np.asarray(L0_attn_x, dtype=np.float32))
    consts = host_consts()
    common = dict(
        delta=np.ascontiguousarray(np.asarray(L0_attn_delta, np.float32).reshape(D, N)),
        alpha=np.ascontiguousarray(np.asarray(L0_attn_alpha, np.float32).reshape(D, N)),
        beta=np.ascontiguousarray(np.asarray(L0_attn_beta, np.float32).reshape(D, N)),
        gamma=np.ascontiguousarray(np.asarray(L0_attn_gamma, np.float32).reshape(D, N)),
        omega=np.ascontiguousarray(np.asarray(L0_attn_omega, np.float32).reshape(1, D)),
        router_w=np.ascontiguousarray(np.asarray(L0_ffn_router_w, np.float32)),
        w1=np.ascontiguousarray(np.asarray(L0_ffn_w1, np.float32).reshape(E * D, HID)),
        b1=np.ascontiguousarray(np.asarray(L0_ffn_b1, np.float32)),
        w2=np.ascontiguousarray(np.asarray(L0_ffn_w2, np.float32).reshape(E * HID, D)),
        b2=np.ascontiguousarray(np.asarray(L0_ffn_b2, np.float32)),
        **consts,
    )
    nc, _ = build_core()
    lower_extended_insts(nc)
    in_maps = []
    for i in range(n_cores):
        m = dict(common)
        m["x"] = np.ascontiguousarray(x[i * B2:(i + 1) * B2].reshape(B2 * S, D))
        in_maps.append(m)
    res = run_bass_kernel_spmd(nc, in_maps, core_ids=list(range(n_cores)))
    out = np.concatenate(
        [r["out"].reshape(B2, S, D) for r in res.results], axis=0)
    return np.ascontiguousarray(out.astype(np.float32))
